# version 01
# baseline (speedup 1.0000x reference)
import numpy as np
import ml_dtypes
from contextlib import ExitStack
import concourse.bass as bass
import concourse.mybir as mybir
from concourse.bass_utils import run_bass_kernel_spmd

F32 = mybir.dt.float32
BF16 = mybir.dt.bfloat16
U8 = mybir.dt.uint8
AF = mybir.ActivationFunctionType
AX = mybir.AxisListType
OP = mybir.AluOpType

B, S, DM, HF, NH, DH = 4, 4096, 1024, 4096, 16, 64
NT = 17
DC = 8
HC = 32
NPAIR = 8
EXPC = 16.0
HALF_COLS = 1152

ENGS = ("sync", "tensor", "vector", "scalar", "gpsimd")
DBG = {}


class Sem:
    __slots__ = ("h", "v")

    def __init__(self, h):
        self.h = h
        self.v = 0


class P:
    def __init__(self, nc, es):
        self.nc = nc
        self.es = es
        self.cmds = {e: [] for e in ENGS}

    def sem(self, name):
        return Sem(self.es.enter_context(self.nc.semaphore(name)))

    def emit(self, eng, fn):
        self.cmds[eng].append(fn)

    def wait(self, eng, sem, val=None):
        v = sem.v if val is None else val
        if v <= 0:
            return
        self.emit(eng, lambda q, h=sem.h, v=v: q.wait_ge(h, v))

    def dma(self, eng, sem, out, in_, **kw):
        sem.v += 16

        def fn(q, out=out, in_=in_, kw=kw, h=sem.h):
            q.dma_start(out=out, in_=in_, **kw).then_inc(h, 16)
        self.emit(eng, fn)
        return sem.v

    def dmat(self, eng, sem, out, in_):
        sem.v += 16

        def fn(q, out=out, in_=in_, h=sem.h):
            q.dma_start_transpose(out, in_).then_inc(h, 16)
        self.emit(eng, fn)
        return sem.v

    def op(self, eng, sem, meth, *args, **kw):
        if eng in ("vector", "scalar", "gpsimd") and sem.v > 0:
            self.wait(eng, sem, sem.v)
        sem.v += 1

        def fn(q, meth=meth, args=args, kw=kw, h=sem.h):
            getattr(q, meth)(*args, **kw).then_inc(h, 1)
        self.emit(eng, fn)
        return sem.v

    def opq(self, eng, meth, *args, **kw):
        def fn(q, meth=meth, args=args, kw=kw):
            getattr(q, meth)(*args, **kw)
        self.emit(eng, fn)

    def checkpoint(self, sems):
        return ({e: len(self.cmds[e]) for e in ENGS},
                [(s, s.v) for s in sems])

    def truncate(self, snap):
        counts, semvals = snap
        for e in ENGS:
            self.cmds[e] = self.cmds[e][:counts[e]]
        for s, v in semvals:
            self.wait("sync", s, v)


def build_program():
    nc = bass.Bass("TRN2", target_bir_lowering=False)
    es = ExitStack()
    p = P(nc, es)

    d_x = nc.dram_tensor("x_own", [NT, 128, DM], F32, kind="ExternalInput").ap()
    d_w1t = nc.dram_tensor("w1t", [DM, HF], F32, kind="ExternalInput").ap()
    d_w2t = nc.dram_tensor("w2t", [HF, DM], F32, kind="ExternalInput").ap()
    d_b1 = nc.dram_tensor("b1p", [128, HC], F32, kind="ExternalInput").ap()
    d_b2 = nc.dram_tensor("b2r", [1, DM], F32, kind="ExternalInput").ap()
    d_wq = nc.dram_tensor("wqp", [DM, DM], F32, kind="ExternalInput").ap()
    d_wk = nc.dram_tensor("wkp", [DM, DM], F32, kind="ExternalInput").ap()
    d_wv = nc.dram_tensor("wvp", [DM, DM], F32, kind="ExternalInput").ap()
    d_wo = nc.dram_tensor("wop", [DM, DM], F32, kind="ExternalInput").ap()
    d_qb = nc.dram_tensor("qbp", [128, NPAIR], F32, kind="ExternalInput").ap()
    d_kb = nc.dram_tensor("kbp", [128, NPAIR], F32, kind="ExternalInput").ap()
    d_vb = nc.dram_tensor("vbr", [1, DM], F32, kind="ExternalInput").ap()
    d_ob = nc.dram_tensor("obr", [1, DM], F32, kind="ExternalInput").ap()
    d_sc = nc.dram_tensor("screp", [128, 1], F32, kind="ExternalInput").ap()
    d_id = nc.dram_tensor("identb", [128, 128], BF16, kind="ExternalInput").ap()
    d_msk = nc.dram_tensor("qkmask", [9, 128, 512], BF16, kind="ExternalInput").ap()
    d_out = nc.dram_tensor("out_own", [16, 128, DM], F32, kind="ExternalOutput").ap()

    def internal(name, shape, dt):
        if name in DBG.get("expose", ()):
            return nc.dram_tensor(name, shape, dt, kind="ExternalOutput").ap()
        return nc.dram_tensor(name, shape, dt).ap()

    d_w1s = internal("w1s", [2, DM, HF], BF16)
    d_w2s = internal("w2s", [2, HF, DM], BF16)
    d_wqs = internal("wqs", [2, DM, DM], BF16)
    d_wks = internal("wks", [2, DM, DM], BF16)
    d_wvs = internal("wvs", [2, DM, DM], BF16)
    d_wos = internal("wos", [2, DM, DM], BF16)
    d_agin = internal("agin", [2, DC, 128, 2048], BF16)
    d_agout = internal("agout", [2, 2, DC, 128, 2048], BF16)
    d_xnown = internal("xnown", [2, DC, 128, NT * 128], BF16)
    d_x1sp = internal("x1sp", [NT, 128, DM], F32)
    d_vtm = internal("vtm", [2, 32, 128, DM], BF16)
    d_avsp = internal("avsp", [2, NPAIR, 128, NT * 128], BF16)
    d_x2d = internal("x2d", [NT, 128, DM], F32)

    sLD = p.sem("sLD")
    sSP = p.sem("sSP")
    sTR = p.sem("sTR")
    sV = p.sem("sV")
    sC = p.sem("sC")
    sT = p.sem("sT")
    sAG = p.sem("sAG")
    sG = p.sem("sG")
    sWS = [p.sem("sWS0"), p.sem("sWS1")]
    sWST = p.sem("sWST")
    sX = p.sem("sX")
    sW1 = [p.sem(f"sW1_{i}") for i in range(4)]
    sW2 = [p.sem(f"sW2_{i}") for i in range(8)]
    sXS = [p.sem("sXS0"), p.sem("sXS1")]
    sVBf = [p.sem("sVB0"), p.sem("sVB1")]
    sWQK = p.sem("sWQK")
    sWVB = p.sem("sWVB")
    sMSK = p.sem("sMSK")
    sOAV = p.sem("sOAV")
    sX1 = [p.sem("sX1_0"), p.sem("sX1_1")]
    sTRid = p.sem("sTRid")
    sTRlt = p.sem("sTRlt")
    sTRg = p.sem("sTRg")
    sTRe = p.sem("sTRe")
    sTRq = p.sem("sTRq")
    sTRt = p.sem("sTRt")

    PS = [es.enter_context(nc.psum_tensor(f"P{i}", [128, 1024], F32))
          for i in range(4)]

    def sbuf(name, shape, dt):
        return es.enter_context(nc.sbuf_tensor(name, shape, dt))

    c_eps = sbuf("c_eps", [128, 1], F32)
    c_expb = sbuf("c_expb", [128, 1], F32)
    c_ones1 = sbuf("c_ones1", [128, 1], BF16)
    c_onesf = sbuf("c_onesf", [128, 64], F32)
    c_onesb = sbuf("c_onesb", [128, 128], BF16)
    identb = sbuf("identb_s", [128, 128], BF16)
    b1p = sbuf("b1p_s", [128, HC], F32)
    qbp = sbuf("qbp_s", [128, NPAIR], F32)
    kbp = sbuf("kbp_s", [128, NPAIR], F32)
    screp = sbuf("screp_s", [128, 1], F32)
    lnsm = sbuf("lnsm", [128, 8], F32)
    srow = sbuf("srow", [128, 512], F32)
    lt_fm = sbuf("lt_fm", [128, DC * 16], F32)
    lt_fm_h = sbuf("lt_fm_h", [128, DC * 16], BF16)
    lt_fm_l = sbuf("lt_fm_l", [128, DC * 16], BF16)

    SMB = 38016
    smb_t = es.enter_context(nc.sbuf_tensor("smalls", [128, SMB], U8))
    smb = smb_t.ap()

    def smv(p0, np_, nbytes, dt, off=0):
        return smb[p0:p0 + np_, off:off + nbytes].bitcast(dt)

    HW = 1152
    identf = smv(0, 16, 64, F32)
    t4f = smv(0, 4, 4 * HW * 4, F32, 64)
    t16f = smv(32, 16, HW * 4, F32)
    t16h = smv(32, 16, HW * 2, BF16, HW * 4)
    t16l = smv(32, 16, HW * 2, BF16, HW * 6)
    gstage = smv(0, 1, DM * 4, F32, 64)
    gq = smv(0, 1, DM * 4, F32, 64 + DM * 4)
    e2d = smv(32, 16, DM * 4, F32, HW * 8 + DM * 4)
    e2h = smv(32, 16, DM * 2, BF16, HW * 8 + DM * 8)
    e2l = smv(32, 16, DM * 2, BF16, HW * 8 + DM * 10)
    obf = smv(32, 1, DM * 4, F32, HW * 8 + DM * 12)
    obb = smv(32, 1, DM * 4, BF16, HW * 8 + DM * 16)
    b2f = smv(32, 1, DM * 4, F32, HW * 8 + DM * 18)
    b2b = smv(32, 1, DM * 4, BF16, HW * 8 + DM * 22)
    e1qf = smv(64, 4, DM * 4, F32)
    e1qh = smv(64, 4, DM * 2, BF16, DM * 4)
    e1ql = smv(64, 4, DM * 2, BF16, DM * 6)
    lt_tm = smv(64, 16, DM * 4, F32, DM * 8)
    p4f = smv(64, 4, HW * 4, F32, DM * 12)
    p4h = smv(64, 4, HW * 2, BF16, DM * 12 + HW * 4)
    p4l = smv(64, 4, HW * 2, BF16, DM * 12 + HW * 6)
    vbf = smv(64, 1, DM * 4, F32, DM * 12 + HW * 8)
    vbb = smv(64, 1, DM * 4, BF16, DM * 16 + HW * 8)
    identf = smv(64, 16, 64, F32, DM * 16 + HW * 8 + DM * 4)
    identf0 = smv(0, 16, 64, F32)

    POOLB = 161 * 1024
    pool_t = es.enter_context(nc.sbuf_tensor("pool", [128, POOLB], U8))
    pl = pool_t.ap()

    def bf(a, b):
        return pl[:, a:b].bitcast(BF16)

    def f32(a, b):
        return pl[:, a:b].bitcast(F32)

    x_res = f32(0, 69632)
    R2 = 69632
    xn_h = bf(R2, R2 + HW * DC * 2)
    xn_l = bf(R2 + HW * DC * 2, R2 + HW * DC * 4)
    F3 = R2 + HW * DC * 4
    w1b = [bf(F3 + i * 2048, F3 + (i + 1) * 2048) for i in range(4)]
    w2b = [bf(F3 + 8192 + i * 2048, F3 + 8192 + (i + 1) * 2048) for i in range(8)]
    F4 = F3 + 24576
    h_h = bf(F4, F4 + 4 * HW * 2)
    h_l = bf(F4 + 4 * HW * 2, F4 + 8 * HW * 2)
    F5 = F4 + 8 * HW * 2
    gel = f32(F5, F5 + 2048)
    ln_stage = f32(F5 + 2048, F5 + 6144)
    ln_hi = bf(F5 + 6144, F5 + 8192)
    ln_lo = bf(F5 + 8192, F5 + 10240)
    sq_scr = f32(F5 + 10240, F5 + 14336)
    FFN_END = F5 + 14336
    assert FFN_END <= POOLB, FFN_END

    A0 = 0
    kf_h = bf(A0, A0 + 8192)
    kf_l = bf(A0 + 8192, A0 + 16384)
    vb_h = bf(A0 + 16384, A0 + 24576)
    vb_l = bf(A0 + 24576, A0 + 32768)
    q_h = bf(A0 + 32768, A0 + 37120)
    q_l = bf(A0 + 37120, A0 + 41472)
    ef32 = [f32(A0 + 41472, A0 + 45568), f32(A0 + 45568, A0 + 49664)]
    ehl = [bf(A0 + 49664, A0 + 53760), bf(A0 + 53760, A0 + 57856)]
    A1 = A0 + 57856
    xs_b = [[bf(A1 + (2 * i + l) * 8192, A1 + (2 * i + l + 1) * 8192)
             for l in range(2)] for i in range(2)]
    A2 = A1 + 32768
    wqk = [[bf(A2 + (2 * i + l) * 2048, A2 + (2 * i + l + 1) * 2048)
            for l in range(2)] for i in range(2)]
    A3 = A2 + 8192
    wvb = [bf(A3 + l * 16384, A3 + (l + 1) * 16384) for l in range(2)]
    A4 = A3 + 32768
    masks = bf(A4, A4 + 9216)
    A5 = A4 + 9216
    av_f = f32(A5, A5 + 2048)
    avn_f = f32(A5 + 2048, A5 + 4096)
    av_h = bf(A5 + 4096, A5 + 8448)
    av_l = bf(A5 + 8448, A5 + 12800)
    A6 = A5 + 12800
    vstage = f32(A6, A6 + 4096)
    vsth = bf(A6 + 4096, A6 + 6144)
    vstl = bf(A6 + 6144, A6 + 8192)
    kqstage = f32(A6 + 8192, A6 + 10240)
    ATT_END = A6 + 10240
    assert ATT_END <= POOLB

    x2_res = f32(0, 69632)
    O1 = 69632
    oav_h = [bf(O1 + i * 4352, O1 + (i + 1) * 4352) for i in range(2)]
    oav_l = [bf(O1 + 8704 + i * 4352, O1 + 8704 + (i + 1) * 4352) for i in range(2)]
    O2 = O1 + 17408
    owo = [[bf(O2 + (2 * i + l) * 2048, O2 + (2 * i + l + 1) * 2048)
            for l in range(2)] for i in range(2)]
    O3 = O2 + 8192
    x1st = [f32(O3 + i * 4096, O3 + (i + 1) * 4096) for i in range(2)]
    O_END = O3 + 8192
    assert O_END <= POOLB

    def barrier():
        vals = [(s, s.v) for s in ([sV, sC, sT, sG, sLD, sSP, sTR, sWST, sX,
                 sWQK, sWVB, sMSK, sOAV, sTRid, sTRlt, sTRg, sTRe, sTRq, sTRt]
                + sWS + sW1 + sW2 + sXS + sVBf + sX1)]
        for eng in ENGS:
            for s, v in vals:
                p.wait(eng, s, v)

    def split(src, hi, lo):
        p.op("vector", sV, "tensor_copy", hi, src)
        p.op("vector", sV, "tensor_tensor", lo, src, hi, OP.subtract)
        return sV.v

    ALLSEMS = ([sLD, sSP, sTR, sV, sC, sT, sG, sWST, sX, sWQK, sWVB, sMSK,
                sOAV, sTRid, sTRlt, sTRg, sTRe, sTRq, sTRt]
               + sWS + sW1 + sW2 + sXS + sVBf + sX1)
    ckpts = {}
    p.op("vector", sV, "memset", c_eps[:], 1e-5)
    p.op("vector", sV, "memset", c_expb[:], -EXPC)
    p.op("vector", sV, "memset", c_ones1[:], 1.0)
    p.op("vector", sV, "memset", c_onesf[:], 1.0)
    p.op("vector", sV, "memset", c_onesb[:], 1.0)

    p.dma("sync", sLD, identb[:], d_id[:])
    p.dma("sync", sLD, b1p[:], d_b1[:])
    p.dma("sync", sLD, qbp[:], d_qb[:])
    p.dma("sync", sLD, kbp[:], d_kb[:])
    p.dma("sync", sLD, screp[:], d_sc[:])
    p.dma("sync", sLD, b2f[0:1, 0:DM], d_b2[:])
    p.dma("sync", sLD, obf[0:1, 0:DM], d_ob[:])
    t = p.dma("sync", sLD, vbf[0:1, 0:DM], d_vb[:])
    CONSTS_LD = sLD.v
    p.wait("vector", sLD, CONSTS_LD)
    p.op("vector", sV, "tensor_copy", identf0[:], identb[0:16, 0:16])
    p.wait("gpsimd", sV, sV.v)
    p.dma("gpsimd", sTRid, identf[:], identf0[:])
    for rf, rbp in ((b2f, b2b), (obf, obb), (vbf, vbb)):
        split(rf[0:1, 0:DM], rbp[0:1, 0:DM], rbp[0:1, DM:2 * DM])

    wsf = [f32(R2, R2 + 16384), f32(R2 + 16384, R2 + 32768)]
    wsh = bf(R2 + 32768, R2 + 40960)
    wsl = bf(R2 + 40960, R2 + 49152)
    jobs = []
    w1v = d_w1t.rearrange("(dc p) h -> dc p h", p=128)
    w1s0 = d_w1s[0].rearrange("(dc p) h -> dc p h", p=128)
    w1s1 = d_w1s[1].rearrange("(dc p) h -> dc p h", p=128)
    for i in range(DC):
        jobs.append((w1v[i], w1s0[i], w1s1[i], HF))
    w2v = d_w2t.rearrange("(hc p) d -> hc p d", p=128)
    w2s0 = d_w2s[0].rearrange("(hc p) d -> hc p d", p=128)
    w2s1 = d_w2s[1].rearrange("(hc p) d -> hc p d", p=128)
    for i in range(HC):
        jobs.append((w2v[i], w2s0[i], w2s1[i], DM))
    for dsrc, ddst in ((d_wq, d_wqs), (d_wk, d_wks), (d_wv, d_wvs), (d_wo, d_wos)):
        sv_ = dsrc.rearrange("(dc p) c -> dc p c", p=128)
        dv0 = ddst[0].rearrange("(dc p) c -> dc p c", p=128)
        dv1 = ddst[1].rearrange("(dc p) c -> dc p c", p=128)
        for i in range(DC):
            jobs.append((sv_[i], dv0[i], dv1[i], DM))
    wv_done, wst_done = [], []
    CKJOBS = DBG.get("njobs", len(jobs))
    jobs = jobs[:CKJOBS]
    for i, (src, dsth, dstl, width) in enumerate(jobs):
        buf = wsf[i % 2]
        if i >= 2:
            p.wait("sync", sV, wv_done[i - 2])
        t = p.dma("sync", sWS[i % 2], buf[:, 0:width], src)
        p.wait("vector", sWS[i % 2], t)
        if i >= 1:
            p.wait("vector", sWST, wst_done[i - 1])
        p.op("vector", sV, "tensor_copy", wsh[:, 0:width], buf[:, 0:width])
        p.op("vector", sV, "tensor_tensor", wsl[:, 0:width], buf[:, 0:width],
             wsh[:, 0:width], OP.subtract)
        wv_done.append(sV.v)
        p.wait("scalar", sV, sV.v)
        p.dma("scalar", sWST, dsth, wsh[:, 0:width])
        wst_done.append(p.dma("scalar", sWST, dstl, wsl[:, 0:width]))

    def emit_ln_tile(x_tile, localslot, transpose, ncols):
        p.wait("vector", sSP, sSP.v)
        ss, qq, mm, vv, rr = (lnsm[:, 0:1], lnsm[:, 1:2], lnsm[:, 2:3],
                              lnsm[:, 3:4], lnsm[:, 4:5])
        p.op("vector", sV, "reduce_sum", ss, x_tile, axis=AX.X)
        p.op("scalar", sC, "activation", sq_scr[:, 0:DM], x_tile, AF.Square,
             accum_out=qq)
        csq = sC.v
        p.wait("vector", sV, sV.v)
        p.op("vector", sV, "tensor_scalar_mul", mm, ss, 1.0 / DM)
        p.op("vector", sV, "tensor_mul", vv, mm, mm)
        p.wait("vector", sC, csq)
        p.op("vector", sV, "tensor_scalar", vv, qq, 1.0 / DM, vv[:, 0:1],
             OP.mult, OP.subtract)
        p.wait("scalar", sV, sV.v)
        p.op("scalar", sC, "activation", rr, vv, AF.Sqrt, bias=c_eps[:, 0:1],
             scale=1.0)
        p.wait("vector", sC, sC.v)
        p.op("vector", sV, "reciprocal", rr, rr)
        p.wait("vector", sV, sV.v)
        p.op("vector", sV, "tensor_scalar", ln_stage[:, 0:DM], x_tile, mm[:, 0:1],
             rr[:, 0:1], OP.subtract, OP.mult)
        split(ln_stage[:, 0:DM], ln_hi[:, 0:DM], ln_lo[:, 0:DM])
        vs = sV.v
        if transpose:
            p.wait("scalar", sV, vs)
            for dc in range(DC):
                p.dmat("scalar", sSP,
                       xn_h[:, dc * ncols + localslot * 128:
                            dc * ncols + localslot * 128 + 128],
                       ln_hi[:, dc * 128:dc * 128 + 128])
                p.dmat("scalar", sSP,
                       xn_l[:, dc * ncols + localslot * 128:
                            dc * ncols + localslot * 128 + 128],
                       ln_lo[:, dc * 128:dc * 128 + 128])
        return vs

    def emit_lora_prep():
        p.wait("gpsimd", sV, sV.v)
        t = p.dma("gpsimd", sTRlt, lt_tm[0:16, 0:DM], ln_stage[1:17, 0:DM])
        p.wait("tensor", sTRlt, t)
        p.wait("tensor", sTRid, 16)
        for dc in range(DC):
            p.op("tensor", sT, "transpose", PS[3][0:128, 512:528],
                 lt_tm[0:16, dc * 128:dc * 128 + 128], identf[:])
            p.wait("vector", sT, sT.v)
            p.op("vector", sV, "tensor_copy", lt_fm[:, dc * 16:dc * 16 + 16],
                 PS[3][0:128, 512:528])
            p.wait("tensor", sV, sV.v)
        split(lt_fm[:], lt_fm_h[:], lt_fm_l[:])
        p.op("vector", sV, "memset", e2d[:], 0.0)
        for i in range(4):
            p.wait("gpsimd", sV, sV.v)
            t_ = p.dma("gpsimd", sTRg, gstage[0:1, 0:DM],
                       lt_tm[8 + 2 * i:8 + 2 * i + 1, 0:DM])
            p.wait("vector", sTRg, t_)
            p.wait("vector", sTRe, sTRe.v)
            p.op("vector", sV, "tensor_copy",
                 gq[0:1, 0:DM].rearrange("p (q r) -> p q r", q=4),
                 gstage[0:1, 0:DM].rearrange("p (r q) -> p q r", q=4))
            p.wait("gpsimd", sV, sV.v)
            p.dma("gpsimd", sTRe,
                  e2d[4 * i:4 * i + 4, i * 256:(i + 1) * 256],
                  gq[0:1, 0:DM].rearrange("p (q r) -> p q r", q=4))
        p.wait("vector", sTRe, sTRe.v)
        split(e2d[:], e2h[:], e2l[:])
        return sV.v

    def emit_ffn_half(slots, out_phase):
        ncols = len(slots) * 128
        if ncols == 1152:
            grps = [(0, 128), (128, 512), (640, 512)]
        else:
            assert ncols == 1024
            grps = [(0, 512), (512, 512)]
        lora_v = sV.v
        for (o, w) in grps:
            p.wait("tensor", sV, lora_v)
            n3 = DC * 3
            k = 0
            for dc in range(DC):
                for pi, (lh, rh) in enumerate(((lt_fm_h, xn_h), (lt_fm_h, xn_l),
                                               (lt_fm_l, xn_h))):
                    k += 1
                    p.op("tensor", sT, "matmul", PS[3][64:68, 512:512 + w],
                         lh[:, dc * 16 + 1:dc * 16 + 9:2],
                         rh[:, dc * ncols + o:dc * ncols + o + w],
                         start=(k == 1), stop=(k == n3),
                         tile_position=(0, 64))
            p.wait("vector", sT, sT.v)
            p.op("vector", sV, "tensor_scalar", p4f[0:4, o:o + w],
                 PS[3][64:68, 512:512 + w], screp[64:68, 0:1], None, OP.mult)
            p.wait("tensor", sV, sV.v)
        split(p4f[0:4, 0:ncols], p4h[0:4, 0:ncols], p4l[0:4, 0:ncols])
        p.op("vector", sV, "memset", t4f[0:4, 0:4 * ncols], 0.0)
        vprep = sV.v
        p.wait("tensor", sV, vprep)

        m1ps = [(PS[0], 0), (PS[0], 512), (PS[1], 0)]
        m1_gelu_done = {}
        m1ctr = 0
        w1_ld, w2_ld = {}, {}
        w1use, w2use = {}, {}
        w1r = d_w1s.rearrange("l (dc p) h -> l p dc h", p=128)
        w2r = d_w2s.rearrange("l (hc p) d -> l hc p d", p=128)

        def load_w1(hc):
            for lvl in range(2):
                slot = (2 * hc + lvl) % 4
                if slot in w1use:
                    p.wait("sync", sT, w1use[slot])
                p.dma("sync", sW1[slot],
                      w1b[slot][:].rearrange("p (dc c) -> p dc c", dc=DC),
                      w1r[lvl][:, :, hc * 128:hc * 128 + 128])
                w1_ld[(hc, lvl)] = (slot, sW1[slot].v)

        def load_w2(hc):
            for lvl in range(2):
                slot = (2 * hc + lvl) % 8
                if slot in w2use:
                    p.wait("sync", sT, w2use[slot])
                p.dma("sync", sW2[slot], w2b[slot][:], w2r[lvl][hc])
                w2_ld[(hc, lvl)] = (slot, sW2[slot].v)

        load_w1(0)
        load_w1(1)
        for hc0 in range(4):
            load_w2(hc0)
        m2ps = [PS[2], PS[3]]
        m2_last = {}

        e1q_v = 0
        NBLK = DBG.get('nblk', 8)
        for blk in range(NBLK):
            if blk % 2 == 0:
                qt = blk // 2
                p.wait("vector", sT, sT.v)
                p.op("vector", sV, "memset", e1qf[:], 0.0)
                p.wait("gpsimd", sV, sV.v)
                t_ = p.dma("gpsimd", sTRq, e1qf[qt:qt + 1, 0:DM],
                           lt_tm[2 * qt:2 * qt + 1, 0:DM])
                p.wait("vector", sTRq, t_)
                split(e1qf[0:4, 0:DM], e1qh[0:4, 0:DM], e1ql[0:4, 0:DM])
                e1q_v = sV.v
            for ci in range(4):
                hc = blk * 4 + ci
                s1h, t1h = w1_ld[(hc, 0)]
                s1l, t1l = w1_ld[(hc, 1)]
                p.wait("tensor", sW1[s1h], t1h)
                p.wait("tensor", sW1[s1l], t1l)
                for (o, w) in grps:
                    ps, po = m1ps[m1ctr % 3]
                    if m1ctr - 3 in m1_gelu_done:
                        p.wait("tensor", sC, m1_gelu_done[m1ctr - 3])
                    nmm = 3 * DC + (0 if DBG.get("no_m1lora") else 3)
                    k = 0
                    for pi, (wsl_, rh) in enumerate(((s1h, xn_h), (s1h, xn_l),
                                                     (s1l, xn_h))):
                        for dc in range(DC):
                            k += 1
                            p.op("tensor", sT, "matmul", ps[0:128, po:po + w],
                                 w1b[wsl_][:, dc * 128:dc * 128 + 128],
                                 rh[:, dc * ncols + o:dc * ncols + o + w],
                                 start=(k == 1), stop=(k == nmm))
                    p.wait("tensor", sV, e1q_v)
                    if DBG.get("no_m1lora"):
                        nmm = 3 * DC
                    for pi, (lh, rh) in enumerate(((e1qh, p4h), (e1qh, p4l),
                                                   (e1ql, p4h))):
                        if DBG.get("no_m1lora"):
                            break
                        k += 1
                        p.op("tensor", sT, "matmul", ps[0:128, po:po + w],
                             lh[0:4, (hc % 8) * 128:(hc % 8) * 128 + 128],
                             rh[0:4, o:o + w], start=False, stop=(k == nmm),
                             tile_position=(64, 0))
                    tmm = sT.v
                    w1use[s1h] = tmm
                    w1use[s1l] = tmm
                    p.wait("scalar", sT, tmm)
                    p.wait("scalar", sV, sV.v)
                    p.op("scalar", sC, "activation", gel[:, 0:w],
                         ps[0:128, po:po + w], AF.Gelu,
                         bias=b1p[:, hc:hc + 1], scale=1.0)
                    m1_gelu_done[m1ctr] = sC.v
                    m1ctr += 1
                    p.wait("vector", sC, sC.v)
                    split(gel[:, 0:w],
                          h_h[:, ci * ncols + o:ci * ncols + o + w],
                          h_l[:, ci * ncols + o:ci * ncols + o + w])
                hs = sV.v
                quarter = hc // 8
                dcl = hc % 8
                p.wait("tensor", sV, hs)
                for (o, w) in grps:
                    for pi, (lh, rh) in enumerate(((lt_fm_h, h_h), (lt_fm_h, h_l),
                                                   (lt_fm_l, h_h))):
                        p.op("tensor", sT, "matmul", PS[1][0:4, 512:512 + w],
                             lh[:, dcl * 16 + 9:dcl * 16 + 16:2],
                             rh[:, ci * ncols + o:ci * ncols + o + w],
                             start=(pi == 0), stop=(pi == 2))
                    p.wait("vector", sT, sT.v)
                    p.op("vector", sV, "scalar_tensor_tensor",
                         t4f[0:4, quarter * ncols + o:quarter * ncols + o + w],
                         PS[1][0:4, 512:512 + w], screp[0:4, 0:1],
                         t4f[0:4, quarter * ncols + o:quarter * ncols + o + w],
                         OP.mult, OP.add)
                    p.wait("tensor", sV, sV.v)
                if hc + 2 < HC:
                    load_w1(hc + 2)
            for ti, sl in enumerate(slots):
                ps = m2ps[ti % 2]
                if ti % 2 in m2_last:
                    p.wait("tensor", sV, m2_last[ti % 2])
                nmm = 4 * 3 * 2
                k = 0
                for ci in range(4):
                    hc = blk * 4 + ci
                    s2h, t2h = w2_ld[(hc, 0)]
                    s2l, t2l = w2_ld[(hc, 1)]
                    if ti == 0:
                        p.wait("tensor", sW2[s2h], t2h)
                        p.wait("tensor", sW2[s2l], t2l)
                    for pi, (hh, wsl_) in enumerate(((h_h, s2h), (h_l, s2h),
                                                     (h_h, s2l))):
                        for half in range(2):
                            k += 1
                            p.op("tensor", sT, "matmul",
                                 ps[0:128, half * 512:half * 512 + 512],
                                 hh[:, ci * ncols + ti * 128:
                                    ci * ncols + ti * 128 + 128],
                                 w2b[wsl_][:, half * 512:half * 512 + 512],
                                 start=(ci == 0 and pi == 0),
                                 stop=(k in (nmm - 1, nmm)))
                    w2use[s2h] = sT.v
                    w2use[s2l] = sT.v
                tmm = sT.v
                p.wait("vector", sT, tmm)
                p.op("vector", sV, "scalar_tensor_tensor",
                     x_res[:, sl * DM:(sl + 1) * DM], ps[0:128, 0:1024], 0.5,
                     x_res[:, sl * DM:(sl + 1) * DM], OP.mult, OP.add)
                m2_last[ti % 2] = sV.v
            if blk + 1 < NBLK:
                for hc2 in range(4 * blk + 4, 4 * blk + 8):
                    load_w2(hc2)

        if DBG.get("no_m2x"):
            return
        NO_T16 = DBG.get("no_t16")
        NO_B2K = DBG.get("no_b2k")
        if not NO_T16:
            p.wait("gpsimd", sV, sV.v)
            for i in range(4):
                for q_ in range(4):
                    p.dma("gpsimd", sTRt,
                          t16f[4 * i + q_:4 * i + q_ + 1, 0:ncols],
                          t4f[i:i + 1, q_ * ncols:(q_ + 1) * ncols])
            p.wait("vector", sTRt, sTRt.v)
            split(t16f[0:16, 0:ncols], t16h[0:16, 0:ncols], t16l[0:16, 0:ncols])
        p.wait("tensor", sV, sV.v)
        for ti, sl in enumerate(slots):
            ps = m2ps[ti % 2]
            if ti % 2 in m2_last:
                p.wait("tensor", sV, m2_last[ti % 2])
            for half in range(2):
                k = 0
                nmm2 = (0 if NO_T16 else 3) + (0 if NO_B2K else 2)
                for pi, (th, eh) in enumerate(((t16h, e2h), (t16h, e2l),
                                               (t16l, e2h))):
                    if NO_T16:
                        break
                    k += 1
                    p.op("tensor", sT, "matmul",
                         ps[0:128, half * 512:half * 512 + 512],
                         th[0:16, ti * 128:ti * 128 + 128],
                         eh[0:16, half * 512:half * 512 + 512],
                         start=(k == 1), stop=(k == nmm2),
                         tile_position=(32, 0))
                for lvl in range(2):
                    if NO_B2K:
                        break
                    k += 1
                    p.op("tensor", sT, "matmul",
                         ps[0:128, half * 512:half * 512 + 512],
                         c_onesb[32:33, 0:128],
                         b2b[0:1, lvl * DM + half * 512:
                             lvl * DM + half * 512 + 512],
                         start=(k == 1), stop=(k == nmm2),
                         tile_position=(32, 0))
            tmm = sT.v
            p.wait("vector", sT, tmm)
            p.op("vector", sV, "scalar_tensor_tensor",
                 x_res[:, sl * DM:(sl + 1) * DM], ps[0:128, 0:1024], 0.5,
                 x_res[:, sl * DM:(sl + 1) * DM], OP.mult, OP.add)
            m2_last[ti % 2] = sV.v
            if out_phase:
                p.wait("scalar", sV, sV.v)
                p.dma("scalar", sSP, d_out[sl - 1],
                      x_res[:, sl * DM:(sl + 1) * DM])

    HALVES1 = [list(range(0, 9)), list(range(9, 17))]

    def emit_ln_ffn(which):
        out_phase = (which == 2)
        for hi_, slots in enumerate(HALVES1):
            if ckpts is not None:
                ckpts[f"pre{which}{hi_}"] = p.checkpoint(ALLSEMS)
            ffn_slots = slots if which == 1 else [s for s in slots if s != 0]
            ncols = len(ffn_slots) * 128
            barrier()
            for li, sl in enumerate(slots):
                do_tr = sl in ffn_slots
                lsl = ffn_slots.index(sl) if do_tr else 0
                emit_ln_tile(x_res[:, sl * DM:(sl + 1) * DM], lsl, do_tr, ncols)
                if sl == 0:
                    emit_lora_prep()
            if ckpts is not None:
                ckpts[f"ln{which}{hi_}"] = p.checkpoint(ALLSEMS)
            p.wait("tensor", sSP, sSP.v)
            emit_ffn_half(ffn_slots, out_phase)
            if ckpts is not None:
                ckpts[f"ffnh{which}{hi_}"] = p.checkpoint(ALLSEMS)

    xld = []
    for tt in range(NT):
        xld.append(p.dma("sync", sX, x_res[:, tt * DM:(tt + 1) * DM], d_x[tt]))
    p.wait("vector", sX, sX.v)
    p.wait("vector", sWST, sWST.v)
    ckpts["wsplit"] = p.checkpoint(ALLSEMS)
    emit_ln_ffn(1)

    barrier()
    p.dma("scalar", sSP, d_x1sp[:].rearrange("t p d -> p t d"),
          x_res[:].rearrange("p (t d) -> p t d", t=NT))
    ckpts["ffn1"] = p.checkpoint(ALLSEMS)
    for hi_, slots in enumerate(HALVES1):
        ncols = len(slots) * 128
        barrier()
        for li, sl in enumerate(slots):
            emit_ln_tile(x_res[:, sl * DM:(sl + 1) * DM], li, True, ncols)

        base = slots[0]
        p.wait("scalar", sSP, sSP.v)
        for dc in range(DC):
            for lvl, src in ((0, xn_h), (1, xn_l)):
                p.dma("scalar", sSP,
                      d_xnown[lvl][dc][:, base * 128:base * 128 + ncols],
                      src[:, dc * ncols:dc * ncols + ncols])
                lo = 1 if base == 0 else base
                off = (lo - base) * 128
                p.dma("scalar", sSP,
                      d_agin[lvl][dc][:, (lo - 1) * 128:(lo - 1) * 128 + ncols - off],
                      src[:, dc * ncols + off:dc * ncols + ncols])
    agin_done = sSP.v
    p.wait("gpsimd", sSP, agin_done)
    sAG.v += 1

    RGROUPS = [[0, 1]] if DBG.get("rg2") else [[0, 1], [2, 3], [4, 5], [6, 7]]

    def fn_ag(q, h=sAG.h):
        q.collective_compute(
            "AllGather", OP.bypass,
            replica_groups=RGROUPS,
            ins=[d_agin.opt()], outs=[d_agout.opt()],
        ).then_inc(h, 1)
    p.emit("gpsimd", fn_ag)
    p.wait("sync", sAG, 1)
    if DBG.get("dump_ag"):
        d_agdbg = nc.dram_tensor("agdbg", [2, 2, DC, 128, 2048], BF16,
                                 kind="ExternalOutput").ap()
        p.dma("sync", sLD, d_agdbg[:], d_agout[:])
    ckpts["ag"] = p.checkpoint(ALLSEMS)

    barrier()

    t = p.dma("sync", sMSK, masks[:].rearrange("p (m c) -> p m c", m=9),
              d_msk[:].rearrange("m p c -> p m c"))
    p.wait("tensor", sMSK, 16)
    wqr = d_wqs.rearrange("l (dc p) c -> l p dc c", p=128)
    wkr = d_wks.rearrange("l (dc p) c -> l p dc c", p=128)
    wvr = d_wvs.rearrange("l (dc p) c -> l p dc c", p=128)
    p.wait("tensor", sAG, 1)
    p.wait("sync", sAG, 1)

    for lvl in range(2):
        p.dma("sync", sWVB, wvb[lvl][:].rearrange("p (dc c) -> p dc c", dc=DC),
              wvr[lvl])
    p.wait("tensor", sWVB, 32)
    agv = d_agout.rearrange("r l dc p c -> r l dc p c")
    xs_ld = {}

    def load_xs(tb, bufi):
        rank, off = tb // 4, (tb % 4) * 512
        for lvl in range(2):
            p.dma(
                "sync", sXS[bufi],
                xs_b[bufi][lvl][:].rearrange("p (dc c) -> p dc c", dc=DC),
                d_agout[rank][lvl][:, :, off:off + 512].rearrange(
                    "dc p c -> p dc c"))
            xs_ld[(tb, lvl)] = (bufi, sXS[bufi].v)

    load_xs(0, 0)
    load_xs(1, 1)
    vwld = sLD.v
    vps = [PS[0], PS[1]]
    vevac = []
    for tb in range(8):
        bufi = tb % 2
        p.wait("tensor", sXS[bufi], xs_ld[(tb, 1)][1])
        for g4 in range(4):
            g = tb * 4 + g4
            ps = vps[g % 2]
            if len(vevac) >= 2:
                p.wait("tensor", sV, vevac[-2])
            for half in range(2):
                k = 0
                for pi, (xl, wl) in enumerate(((0, 0), (1, 0), (0, 1))):
                    for dc in range(DC):
                        k += 1
                        p.op("tensor", sT, "matmul",
                             ps[0:128, half * 512:half * 512 + 512],
                             xs_b[bufi][xl][:, dc * 512 + g4 * 128:
                                            dc * 512 + g4 * 128 + 128],
                             wvb[wl][:, dc * 1024 + half * 512:
                                     dc * 1024 + half * 512 + 512],
                             start=(k == 1), stop=False)
                for lvl in range(2):
                    p.op("tensor", sT, "matmul",
                         ps[0:128, half * 512:half * 512 + 512],
                         c_onesb[64:65, 0:128],
                         vbb[0:1, lvl * DM + half * 512:
                             lvl * DM + half * 512 + 512],
                         start=False, stop=(lvl == 1), tile_position=(64, 0))
            tmm = sT.v
            p.wait("vector", sT, tmm)
            p.wait("vector", sSP, sSP.v)
            p.op("vector", sV, "tensor_copy", vstage[:, 0:1024], ps[0:128, 0:1024])
            split(vstage[:, 0:1024], vsth[:, 0:1024], vstl[:, 0:1024])
            vevac.append(sV.v)
            p.wait("scalar", sV, sV.v)
            p.dma("scalar", sSP, d_vtm[0][g], vsth[:, 0:1024])
            p.dma("scalar", sSP, d_vtm[1][g], vstl[:, 0:1024])
        if tb + 2 < 8:
            p.wait("sync", sT, sT.v)
            load_xs(tb + 2, bufi)
    vspill_done = sSP.v
    ckpts["vproj"] = p.checkpoint(ALLSEMS)

    QGRPS = [(0, 128), (128, 512), (640, 512), (1152, 512), (1664, 512)]

    def klist_for(j):
        ks = []
        for g in range(4 * j + 4):
            ks.append((g, (g - 4 * j) if g >= 4 * j else None))
        for g in range(4 * j + 4):
            ks.append((16 + g, (4 + g - 4 * j) if g >= 4 * j else None))
        return ks

    vfetch_ranges = {
        "sh": [(0, 1)],
        0: [(1, 4), (16, 20)],
        1: [(4, 8), (20, 24)],
        2: [(8, 12), (24, 28)],
        3: [(12, 16), (28, 32)],
    }
    av_last_read = 0
    kq_last_use = 0
    vb_last_use = 0
    p.wait("sync", sSP, vspill_done)
    for pi_ in range(NPAIR):
        wq_ld = []
        for mi, wsrc in ((0, wqr), (1, wkr)):
            for lvl in range(2):
                wq_ld.append(p.dma(
                    "sync", sWQK,
                    wqk[mi][lvl][:].rearrange("p (dc c) -> p dc c", dc=DC),
                    wsrc[lvl][:, :, pi_ * 128:pi_ * 128 + 128]))
        kq_last = {}
        for tb in range(8):
            bufi = tb % 2
            p.wait("sync", sT, sT.v)
            load_xs(tb, bufi)
            p.wait("tensor", sXS[bufi], xs_ld[(tb, 1)][1])
            p.wait("tensor", sWQK, wq_ld[-1])
            if ("k", tb % 2) in kq_last:
                p.wait("tensor", sV, kq_last[("k", tb % 2)])
            ps = PS[2][0:128, (tb % 2) * 512:(tb % 2) * 512 + 512]
            k = 0
            for xl, wl in ((0, 0), (1, 0), (0, 1)):
                for dc in range(DC):
                    k += 1
                    p.op("tensor", sT, "matmul", ps,
                         wqk[1][wl][:, dc * 128:dc * 128 + 128],
                         xs_b[bufi][xl][:, dc * 512:dc * 512 + 512],
                         start=(k == 1), stop=(k == 24))
            tmm = sT.v
            p.wait("vector", sT, tmm)
            p.op("vector", sV, "tensor_scalar", kqstage[:, 0:512], ps,
                 kbp[:, pi_:pi_ + 1], None, OP.add)
            split(kqstage[:, 0:512], kf_h[:, tb * 512:tb * 512 + 512],
                  kf_l[:, tb * 512:tb * 512 + 512])
            kq_last[("k", tb % 2)] = sV.v
        for gi, (o, w) in enumerate(QGRPS):
            bufi = gi % 2
            p.wait("sync", sT, sT.v)
            for lvl in range(2):
                p.dma(
                    "sync", sXS[bufi],
                    xs_b[bufi][lvl][:, 0:DC * 512].rearrange(
                        "p (dc c) -> p dc c", dc=DC)[:, :, 0:w],
                    d_xnown[lvl][:, :, o:o + w].rearrange("dc p c -> p dc c"))
            p.wait("tensor", sXS[bufi], sXS[bufi].v)
            if ("q", gi % 2) in kq_last:
                p.wait("tensor", sV, kq_last[("q", gi % 2)])
            ps = PS[3][0:128, (gi % 2) * 512:(gi % 2) * 512 + w]
            k = 0
            for xl, wl in ((0, 0), (1, 0), (0, 1)):
                for dc in range(DC):
                    k += 1
                    p.op("tensor", sT, "matmul", ps,
                         wqk[0][wl][:, dc * 128:dc * 128 + 128],
                         xs_b[bufi][xl][:, dc * 512:dc * 512 + w],
                         start=(k == 1), stop=(k == 24))
            tmm = sT.v
            p.wait("vector", sT, tmm)
            p.op("vector", sV, "tensor_scalar", kqstage[:, 0:w], ps,
                 qbp[:, pi_:pi_ + 1], None, OP.add)
            split(kqstage[:, 0:w], q_h[:, o:o + w], q_l[:, o:o + w])
            kq_last[("q", gi % 2)] = sV.v
        kq_done_v = sV.v

        p.wait("tensor", sV, kq_done_v)
        p.wait("tensor", sV, av_last_read)
        fetched = set()
        qblocks = [("sh", 0, 128, [(0, 8)])]
        for j in range(4):
            qblocks.append((j, (4 * j + 1) * 128, 512, klist_for(j)))
        av_read_v = av_last_read
        exp_hist = {}
        eready_v = {}
        eready_g = {}
        avsum_hist = {}
        for qi, (qb, qo, qw, klist) in enumerate(qblocks):
            p.wait("gpsimd", sT, sT.v)
            p.wait("vector", sT, sT.v)
            p.wait("sync", sT, vb_last_use)
            for (g0, g1) in vfetch_ranges[qb]:
                for lvl, dst in ((0, vb_h), (1, vb_l)):
                    p.dma("sync", sVBf[qi % 2], dst[:, g0 * 128:g1 * 128].rearrange(
                        "p (g c) -> p g c", g=g1 - g0),
                        d_vtm[lvl][g0:g1, :, pi_ * 128:pi_ * 128 + 128].rearrange(
                            "g p c -> p g c"))
            p.wait("tensor", sVBf[qi % 2], sVBf[qi % 2].v)
            nk = len(klist)
            for t, (g, mslot) in enumerate(klist):
                grp = PS[t % 2]
                if t - 2 in exp_hist:
                    p.wait("tensor", sC, exp_hist[t - 2])
                elif t < 2 and qi > 0:
                    p.wait("tensor", sC, sC.v)
                for hd in range(2):
                    reg = grp[0:128, hd * 512:hd * 512 + qw]
                    k = 0
                    for kx, qx in ((kf_h, q_h), (kf_h, q_l), (kf_l, q_h)):
                        k += 1
                        p.op("tensor", sT, "matmul", reg,
                             kx[hd * 64:hd * 64 + 64, g * 128:g * 128 + 128],
                             qx[hd * 64:hd * 64 + 64, qo:qo + qw],
                             start=(k == 1), stop=(k == 3 and mslot is None),
                             tile_position=(hd * 64, 0))
                    if mslot is not None:
                        p.op("tensor", sT, "matmul", reg,
                             identb[:],
                             masks[:, mslot * 512:mslot * 512 + qw],
                             start=False, stop=True)
                tqk = sT.v
                p.wait("scalar", sT, tqk)
                if t - 2 in eready_v:
                    p.wait("scalar", sV, eready_v[t - 2])
                    p.wait("scalar", sG, eready_g[t - 2])
                for hd in range(2):
                    p.op("scalar", sC, "activation",
                         ef32[t % 2][:, hd * 512:hd * 512 + qw],
                         grp[0:128, hd * 512:hd * 512 + qw], AF.Exp,
                         bias=c_expb[:, 0:1], scale=0.125)
                exp_hist[t] = sC.v
                p.wait("gpsimd", sC, exp_hist[t])
                if t - 2 in avsum_hist:
                    p.wait("gpsimd", sT, avsum_hist[t - 2])
                    p.wait("vector", sT, avsum_hist[t - 2])
                for hd in range(2):
                    p.op("gpsimd", sG, "tensor_copy",
                         ehl[t % 2][:, hd * 512:hd * 512 + qw],
                         ef32[t % 2][:, hd * 512:hd * 512 + qw])
                eready_g[t] = sG.v
                p.wait("vector", sC, exp_hist[t])
                p.wait("vector", sG, eready_g[t])
                for hd in range(2):
                    p.op("vector", sV, "tensor_tensor",
                         ehl[t % 2][:, 1024 + hd * 512:1024 + hd * 512 + qw],
                         ef32[t % 2][:, hd * 512:hd * 512 + qw],
                         ehl[t % 2][:, hd * 512:hd * 512 + qw], OP.subtract)
                eready_v[t] = sV.v
                p.wait("tensor", sG, eready_g[t])
                p.wait("tensor", sV, eready_v[t])
                if t == 0:
                    p.wait("tensor", sV, av_read_v)
                for hd in range(2):
                    k = 0
                    avreg = (PS[2][0:64, 0:qw] if hd == 0 else
                             PS[2][64:128, 512:512 + qw])
                    for vx, eoff in ((vb_h, 0), (vb_l, 0), (vb_h, 1024)):
                        k += 1
                        p.op("tensor", sT, "matmul", avreg,
                             vx[:, g * 128 + hd * 64:g * 128 + hd * 64 + 64],
                             ehl[t % 2][:, eoff + hd * 512:eoff + hd * 512 + qw],
                             start=(t == 0 and k == 1),
                             stop=(t == nk - 1 and k == 3),
                             tile_position=(0, hd * 64))
                    for li_, eoff in ((0, 0), (1, 1024)):
                        p.op("tensor", sT, "matmul",
                             PS[3][hd * 64:hd * 64 + 1, 0:qw],
                             c_ones1[:, 0:1],
                             ehl[t % 2][:, eoff + hd * 512:eoff + hd * 512 + qw],
                             start=(t == 0 and li_ == 0),
                             stop=(t == nk - 1 and li_ == 1),
                             tile_position=(0, hd * 64))
                avsum_hist[t] = sT.v
            vb_last_use = sT.v
            core_done = sT.v
            p.wait("vector", sT, core_done)
            p.op("vector", sV, "tensor_copy", av_f[0:64, 0:qw], PS[2][0:64, 0:qw])
            p.op("vector", sV, "tensor_copy", av_f[64:128, 0:qw],
                 PS[2][64:128, 512:512 + qw])
            p.op("vector", sV, "tensor_copy", srow[0:1, 0:qw], PS[3][0:1, 0:qw])
            p.op("vector", sV, "tensor_copy", srow[64:65, 0:qw], PS[3][64:65, 0:qw])
            p.op("vector", sV, "reciprocal", srow[0:1, 0:qw], srow[0:1, 0:qw])
            p.op("vector", sV, "reciprocal", srow[64:65, 0:qw], srow[64:65, 0:qw])
            vrec = sV.v
            p.wait("tensor", sV, vrec)
            for hd in range(2):
                p.op("tensor", sT, "matmul", PS[3][hd * 64:hd * 64 + 64,
                                                   512:512 + qw],
                     c_onesf[hd * 64:hd * 64 + 1, 0:64],
                     srow[hd * 64:hd * 64 + 1, 0:qw],
                     start=True, stop=True, tile_position=(hd * 64, hd * 64))
            trep = sT.v
            p.wait("vector", sT, trep)
            p.op("vector", sV, "tensor_tensor", avn_f[:, 0:qw], av_f[:, 0:qw],
                 PS[3][0:128, 512:512 + qw], OP.mult)
            if qi == 0:
                p.wait("vector", sSP, sSP.v)
            split(avn_f[:, 0:qw], av_h[:, qo:qo + qw], av_l[:, qo:qo + qw])
            av_read_v = sV.v
        av_last_read = av_read_v
        p.wait("scalar", sV, av_read_v)
        p.dma("scalar", sSP, d_avsp[0][pi_], av_h[:])
        p.dma("scalar", sSP, d_avsp[1][pi_], av_l[:])
    avsp_done = sSP.v
    ckpts["att"] = p.checkpoint(ALLSEMS)

    barrier()
    wor = d_wos
    x1r = d_x1sp
    evac2 = []
    o_last = {}
    for duo in range(4):
        ldv = []
        p.wait("sync", sT, sT.v)
        for i in range(2):
            pi_ = 2 * duo + i
            for lvl in range(2):
                ldv.append(p.dma("sync", sOAV, oav_h[i][:] if lvl == 0 else
                                 oav_l[i][:], d_avsp[lvl][pi_]))
                ldv.append(p.dma("sync", sOAV, owo[i][lvl][:],
                                 wor[lvl][pi_ * 128:pi_ * 128 + 128, :]))
        for ti in range(NT):
            ps = PS[ti % 2]
            if duo == 0:
                if ("x1", ti % 2) in o_last:
                    p.wait("sync", sV, o_last[("x1", ti % 2)])
                x1ld = p.dma("sync", sX1[ti % 2], x1st[ti % 2][:, 0:DM], x1r[ti])
            if ti % 2 in o_last:
                p.wait("tensor", sV, o_last[ti % 2])
            p.wait("tensor", sOAV, ldv[-1])
            for half in range(2):
                k = 0
                last = 6 + (2 if duo == 0 else 0)
                for i in range(2):
                    for ax, wl in ((oav_h[i], 0), (oav_l[i], 0), (oav_h[i], 1)):
                        k += 1
                        p.op("tensor", sT, "matmul",
                             ps[0:128, half * 512:half * 512 + 512],
                             ax[:, ti * 128:ti * 128 + 128],
                             owo[i][wl][:, half * 512:half * 512 + 512],
                             start=(k == 1), stop=(k == last))
                if duo == 0:
                    for lvl in range(2):
                        k += 1
                        p.op("tensor", sT, "matmul",
                             ps[0:128, half * 512:half * 512 + 512],
                             c_onesb[32:33, 0:128],
                             obb[0:1, lvl * DM + half * 512:
                                 lvl * DM + half * 512 + 512],
                             start=False, stop=(k == last),
                             tile_position=(32, 0))
            tmm = sT.v
            p.wait("vector", sT, tmm)
            if duo == 0:
                p.wait("vector", sX1[ti % 2], x1ld)
                p.op("vector", sV, "tensor_tensor",
                     x2_res[:, ti * DM:(ti + 1) * DM], ps[0:128, 0:1024],
                     x1st[ti % 2][:, 0:DM], OP.add)
            else:
                p.op("vector", sV, "tensor_tensor",
                     x2_res[:, ti * DM:(ti + 1) * DM], ps[0:128, 0:1024],
                     x2_res[:, ti * DM:(ti + 1) * DM], OP.add)
            o_last[ti % 2] = sV.v
            o_last[("x1", ti % 2)] = sV.v

    if "x2d" in DBG.get("expose", ()):
        p.wait("scalar", sV, sV.v)
        p.dma("scalar", sSP, d_x2d[:].rearrange("t p d -> p t d"),
              x2_res[:].rearrange("p (t d) -> p t d", t=NT))

    ckpts["o"] = p.checkpoint(ALLSEMS)
    barrier()
    emit_ln_ffn(2)
    p.wait("sync", sSP, sSP.v)
    p.wait("sync", sV, sV.v)
    if DBG.get("stop") in ckpts:
        p.truncate(ckpts[DBG["stop"]])

    with nc.Block() as block:
        for eng in ENGS:
            def mk(e):
                def run(q):
                    for fn in p.cmds[e]:
                        fn(q)
                return run
            getattr(block, eng)(mk(eng))
    return nc, es


def _pack_inputs(inputs, c):
    rb = c % 2
    b = c // 2
    x = np.asarray(inputs["L0_attn_x"])
    tiles = x[b].reshape(32, 128, DM)
    x_own = np.ascontiguousarray(
        np.concatenate([tiles[0:1], tiles[rb::2]], axis=0))
    W1 = np.asarray(inputs["L0_ffn_W1"])
    W2 = np.asarray(inputs["L0_ffn_W2"])
    Qw = np.asarray(inputs["L0_attn_Q_w"])
    Kw = np.asarray(inputs["L0_attn_K_w"])
    Vw = np.asarray(inputs["L0_attn_V_w"])
    Ow = np.asarray(inputs["L0_attn_O_w"])
    m = {
        "x_own": x_own,
        "w1t": np.ascontiguousarray(W1.T),
        "w2t": np.ascontiguousarray(W2.T),
        "b1p": np.ascontiguousarray(
            np.asarray(inputs["L0_ffn_b1"]).reshape(32, 128).T),
        "b2r": np.asarray(inputs["L0_ffn_b2"]).reshape(1, DM),
        "wqp": np.ascontiguousarray(Qw.transpose(1, 0, 2).reshape(DM, DM)),
        "wkp": np.ascontiguousarray(Kw.transpose(1, 0, 2).reshape(DM, DM)),
        "wvp": np.ascontiguousarray(Vw.transpose(1, 0, 2).reshape(DM, DM)),
        "wop": np.ascontiguousarray(Ow.reshape(DM, DM)),
        "qbp": np.ascontiguousarray(
            np.asarray(inputs["L0_attn_Q_b"]).reshape(8, 128).T),
        "kbp": np.ascontiguousarray(
            np.asarray(inputs["L0_attn_K_b"]).reshape(8, 128).T),
        "vbr": np.asarray(inputs["L0_attn_V_b"]).reshape(1, DM),
        "obr": np.asarray(inputs["L0_attn_O_b"]).reshape(1, DM),
        "screp": np.ascontiguousarray(np.broadcast_to(
            np.asarray(inputs["L0_ffn_scale"]).reshape(1, 1), (128, 1))),
        "identb": np.eye(128, dtype=ml_dtypes.bfloat16),
        "qkmask": _mk_masks(rb),
    }
    return m


def _mk_masks(rb):
    kk = np.arange(128).reshape(128, 1)
    qq = np.arange(128).reshape(1, 128)
    msk = np.zeros((9, 128, 512), np.float32)
    for run in range(2):
        for i in range(4):
            slot = run * 4 + i
            for ip in range(4):
                kglob = 128 * (2 * i + run) + kk
                qglob = 128 * (2 * ip + rb) + qq
                msk[slot][:, ip * 128:(ip + 1) * 128] = np.where(
                    kglob > qglob, NEGM, 0.0)
    msk[8][:, 0:128] = np.where(kk > qq, NEGM, 0.0)
    return msk.astype(ml_dtypes.bfloat16)


NEGM = -100000.0
_CACHED = {}


def kernel(**inputs):
    if "nc" not in _CACHED:
        _CACHED["nc"], _CACHED["es"] = build_program()
    nc = _CACHED["nc"]
    in_maps = [_pack_inputs(inputs, c) for c in range(8)]
    res = run_bass_kernel_spmd(nc, in_maps, core_ids=list(range(8)))
    out = np.empty((B, S, DM), np.float32)
    for c in range(8):
        rb, b = c % 2, c // 2
        out[b].reshape(32, 128, DM)[rb::2] = np.asarray(
            res.results[c]["out_own"])
    return out



# version 02
# speedup vs baseline: 1.5183x; 1.5183x over previous
import numpy as np
import ml_dtypes
from contextlib import ExitStack
import concourse.bass as bass
import concourse.mybir as mybir
from concourse.bass_utils import run_bass_kernel_spmd

F32 = mybir.dt.float32
BF16 = mybir.dt.bfloat16
U8 = mybir.dt.uint8
AF = mybir.ActivationFunctionType
AX = mybir.AxisListType
OP = mybir.AluOpType

B, S, DM, HF, NH, DH = 4, 4096, 1024, 4096, 16, 64
NT = 17
DC = 8
HC = 32
NPAIR = 8
EXPC = 16.0
HALF_COLS = 1152

ENGS = ("sync", "tensor", "vector", "scalar", "gpsimd")
DBG = {}


class Sem:
    __slots__ = ("h", "v")

    def __init__(self, h):
        self.h = h
        self.v = 0


class P:
    def __init__(self, nc, es):
        self.nc = nc
        self.es = es
        self.cmds = {e: [] for e in ENGS}

    def sem(self, name):
        return Sem(self.es.enter_context(self.nc.semaphore(name)))

    def emit(self, eng, fn):
        self.cmds[eng].append(fn)

    def wait(self, eng, sem, val=None):
        v = sem.v if val is None else val
        if v <= 0:
            return
        self.emit(eng, lambda q, h=sem.h, v=v: q.wait_ge(h, v))

    def dma(self, eng, sem, out, in_, **kw):
        sem.v += 16

        def fn(q, out=out, in_=in_, kw=kw, h=sem.h):
            q.dma_start(out=out, in_=in_, **kw).then_inc(h, 16)
        self.emit(eng, fn)
        return sem.v

    def dmat(self, eng, sem, out, in_):
        sem.v += 16

        def fn(q, out=out, in_=in_, h=sem.h):
            q.dma_start_transpose(out, in_).then_inc(h, 16)
        self.emit(eng, fn)
        return sem.v

    def op(self, eng, sem, meth, *args, **kw):
        if eng in ("vector", "scalar", "gpsimd") and sem.v > 0:
            self.wait(eng, sem, sem.v)
        sem.v += 1

        def fn(q, meth=meth, args=args, kw=kw, h=sem.h):
            getattr(q, meth)(*args, **kw).then_inc(h, 1)
        self.emit(eng, fn)
        return sem.v

    def opq(self, eng, meth, *args, **kw):
        def fn(q, meth=meth, args=args, kw=kw):
            getattr(q, meth)(*args, **kw)
        self.emit(eng, fn)

    def checkpoint(self, sems):
        return ({e: len(self.cmds[e]) for e in ENGS},
                [(s, s.v) for s in sems])

    def truncate(self, snap):
        counts, semvals = snap
        for e in ENGS:
            self.cmds[e] = self.cmds[e][:counts[e]]
        for s, v in semvals:
            self.wait("sync", s, v)


def build_program():
    nc = bass.Bass("TRN2", target_bir_lowering=False)
    es = ExitStack()
    p = P(nc, es)

    d_x = nc.dram_tensor("x_own", [NT, 128, DM], F32, kind="ExternalInput").ap()
    d_w1t = nc.dram_tensor("w1t", [DM, HF], F32, kind="ExternalInput").ap()
    d_w2t = nc.dram_tensor("w2t", [HF, DM], F32, kind="ExternalInput").ap()
    d_b1 = nc.dram_tensor("b1p", [128, HC], F32, kind="ExternalInput").ap()
    d_b2 = nc.dram_tensor("b2r", [1, DM], F32, kind="ExternalInput").ap()
    d_wq = nc.dram_tensor("wqp", [DM, DM], F32, kind="ExternalInput").ap()
    d_wk = nc.dram_tensor("wkp", [DM, DM], F32, kind="ExternalInput").ap()
    d_wv = nc.dram_tensor("wvp", [DM, DM], F32, kind="ExternalInput").ap()
    d_wo = nc.dram_tensor("wop", [DM, DM], F32, kind="ExternalInput").ap()
    d_qb = nc.dram_tensor("qbp", [128, NPAIR], F32, kind="ExternalInput").ap()
    d_kb = nc.dram_tensor("kbp", [128, NPAIR], F32, kind="ExternalInput").ap()
    d_vb = nc.dram_tensor("vbr", [1, DM], F32, kind="ExternalInput").ap()
    d_ob = nc.dram_tensor("obr", [1, DM], F32, kind="ExternalInput").ap()
    d_sc = nc.dram_tensor("screp", [128, 1], F32, kind="ExternalInput").ap()
    d_id = nc.dram_tensor("identb", [128, 128], BF16, kind="ExternalInput").ap()
    d_msk = nc.dram_tensor("qkmask", [9, 128, 512], BF16, kind="ExternalInput").ap()
    d_out = nc.dram_tensor("out_own", [16, 128, DM], F32, kind="ExternalOutput").ap()

    def internal(name, shape, dt):
        if name in DBG.get("expose", ()):
            return nc.dram_tensor(name, shape, dt, kind="ExternalOutput").ap()
        return nc.dram_tensor(name, shape, dt).ap()

    d_w1s = internal("w1s", [2, DM, HF], BF16)
    d_w2s = internal("w2s", [2, HF, DM], BF16)
    d_wqs = internal("wqs", [2, DM, DM], BF16)
    d_wks = internal("wks", [2, DM, DM], BF16)
    d_wvs = internal("wvs", [2, DM, DM], BF16)
    d_wos = internal("wos", [2, DM, DM], BF16)
    d_agin = internal("agin", [2, DC, 128, 2048], BF16)
    d_agout = internal("agout", [2, 2, DC, 128, 2048], BF16)
    d_xnown = internal("xnown", [2, DC, 128, NT * 128], BF16)
    d_x1sp = internal("x1sp", [NT, 128, DM], F32)
    d_vtm = internal("vtm", [2, 32, 128, DM], BF16)
    d_avsp = internal("avsp", [2, NPAIR, 128, NT * 128], BF16)
    d_x2d = internal("x2d", [NT, 128, DM], F32)

    sLD = p.sem("sLD")
    sSP = p.sem("sSP")
    sTR = p.sem("sTR")
    sV = p.sem("sV")
    sC = p.sem("sC")
    sT = p.sem("sT")
    sAG = p.sem("sAG")
    sG = p.sem("sG")
    sWS = [p.sem("sWS0"), p.sem("sWS1")]
    sWST = p.sem("sWST")
    sX = p.sem("sX")
    sW1 = [p.sem(f"sW1_{i}") for i in range(4)]
    sW2 = [p.sem(f"sW2_{i}") for i in range(8)]
    sXS = [p.sem("sXS0"), p.sem("sXS1")]
    sVBf = [p.sem("sVB0"), p.sem("sVB1")]
    sWQK = p.sem("sWQK")
    sWVB = p.sem("sWVB")
    sMSK = p.sem("sMSK")
    sOAV = p.sem("sOAV")
    sX1 = [p.sem("sX1_0"), p.sem("sX1_1")]
    sTRid = p.sem("sTRid")
    sTRlt = p.sem("sTRlt")
    sTRg = p.sem("sTRg")
    sTRe = p.sem("sTRe")
    sTRq = p.sem("sTRq")
    sTRt = p.sem("sTRt")

    PS = [es.enter_context(nc.psum_tensor(f"P{i}", [128, 1024], F32))
          for i in range(4)]

    def sbuf(name, shape, dt):
        return es.enter_context(nc.sbuf_tensor(name, shape, dt))

    c_eps = sbuf("c_eps", [128, 1], F32)
    c_expb = sbuf("c_expb", [128, 1], F32)
    c_ones1 = sbuf("c_ones1", [128, 1], BF16)
    c_onesf = sbuf("c_onesf", [128, 64], F32)
    c_onesb = sbuf("c_onesb", [128, 128], BF16)
    identb = sbuf("identb_s", [128, 128], BF16)
    b1p = sbuf("b1p_s", [128, HC], F32)
    qbp = sbuf("qbp_s", [128, NPAIR], F32)
    kbp = sbuf("kbp_s", [128, NPAIR], F32)
    screp = sbuf("screp_s", [128, 1], F32)
    lnsm = sbuf("lnsm", [128, 8], F32)
    srow = sbuf("srow", [128, 512], F32)
    lt_fm = sbuf("lt_fm", [128, DC * 16], F32)
    lt_fm_h = sbuf("lt_fm_h", [128, DC * 16], BF16)
    lt_fm_l = sbuf("lt_fm_l", [128, DC * 16], BF16)

    SMB = 38016
    smb_t = es.enter_context(nc.sbuf_tensor("smalls", [128, SMB], U8))
    smb = smb_t.ap()

    def smv(p0, np_, nbytes, dt, off=0):
        return smb[p0:p0 + np_, off:off + nbytes].bitcast(dt)

    HW = 1152
    identf = smv(0, 16, 64, F32)
    t4f = smv(0, 4, 4 * HW * 4, F32, 64)
    t16f = smv(32, 16, HW * 4, F32)
    t16h = smv(32, 16, HW * 2, BF16, HW * 4)
    t16l = smv(32, 16, HW * 2, BF16, HW * 6)
    gstage = smv(0, 1, DM * 4, F32, 64)
    gq = smv(0, 1, DM * 4, F32, 64 + DM * 4)
    e2d = smv(32, 16, DM * 4, F32, HW * 8 + DM * 4)
    e2h = smv(32, 16, DM * 2, BF16, HW * 8 + DM * 8)
    e2l = smv(32, 16, DM * 2, BF16, HW * 8 + DM * 10)
    obf = smv(32, 1, DM * 4, F32, HW * 8 + DM * 12)
    obb = smv(32, 1, DM * 4, BF16, HW * 8 + DM * 16)
    b2f = smv(32, 1, DM * 4, F32, HW * 8 + DM * 18)
    b2b = smv(32, 1, DM * 4, BF16, HW * 8 + DM * 22)
    e1qf = smv(64, 4, DM * 4, F32)
    e1qh = smv(64, 4, DM * 2, BF16, DM * 4)
    e1ql = smv(64, 4, DM * 2, BF16, DM * 6)
    lt_tm = smv(64, 16, DM * 4, F32, DM * 8)
    p4f = smv(64, 4, HW * 4, F32, DM * 12)
    p4h = smv(64, 4, HW * 2, BF16, DM * 12 + HW * 4)
    p4l = smv(64, 4, HW * 2, BF16, DM * 12 + HW * 6)
    vbf = smv(64, 1, DM * 4, F32, DM * 12 + HW * 8)
    vbb = smv(64, 1, DM * 4, BF16, DM * 16 + HW * 8)
    identf = smv(64, 16, 64, F32, DM * 16 + HW * 8 + DM * 4)
    identf0 = smv(0, 16, 64, F32)

    POOLB = 161 * 1024
    pool_t = es.enter_context(nc.sbuf_tensor("pool", [128, POOLB], U8))
    pl = pool_t.ap()

    def bf(a, b):
        return pl[:, a:b].bitcast(BF16)

    def f32(a, b):
        return pl[:, a:b].bitcast(F32)

    x_res = f32(0, 69632)
    R2 = 69632
    xn_h = bf(R2, R2 + HW * DC * 2)
    xn_l = bf(R2 + HW * DC * 2, R2 + HW * DC * 4)
    F3 = R2 + HW * DC * 4
    w1b = [bf(F3 + i * 2048, F3 + (i + 1) * 2048) for i in range(4)]
    w2b = [bf(F3 + 8192 + i * 2048, F3 + 8192 + (i + 1) * 2048) for i in range(8)]
    F4 = F3 + 24576
    h_h = bf(F4, F4 + 4 * HW * 2)
    h_l = bf(F4 + 4 * HW * 2, F4 + 8 * HW * 2)
    F5 = F4 + 8 * HW * 2
    gel = f32(F5, F5 + 2048)
    ln_stage = f32(F5 + 2048, F5 + 6144)
    ln_hi = bf(F5 + 6144, F5 + 8192)
    ln_lo = bf(F5 + 8192, F5 + 10240)
    sq_scr = f32(F5 + 10240, F5 + 14336)
    FFN_END = F5 + 14336
    assert FFN_END <= POOLB, FFN_END

    A0 = 0
    kf_h = bf(A0, A0 + 8192)
    kf_l = bf(A0 + 8192, A0 + 16384)
    vb_h = bf(A0 + 16384, A0 + 24576)
    vb_l = bf(A0 + 24576, A0 + 32768)
    q_h = bf(A0 + 32768, A0 + 37120)
    q_l = bf(A0 + 37120, A0 + 41472)
    ef32 = [f32(A0 + 41472, A0 + 45568), f32(A0 + 45568, A0 + 49664), None]
    ehl = [bf(A0 + 49664, A0 + 53760), bf(A0 + 53760, A0 + 57856), None]
    A1 = A0 + 57856
    xs_b = [[bf(A1 + (2 * i + l) * 8192, A1 + (2 * i + l + 1) * 8192)
             for l in range(2)] for i in range(2)]
    A2 = A1 + 32768
    wqk = [[bf(A2 + (2 * i + l) * 2048, A2 + (2 * i + l + 1) * 2048)
            for l in range(2)] for i in range(2)]
    A3 = A2 + 8192
    wvb = [bf(A3 + l * 16384, A3 + (l + 1) * 16384) for l in range(2)]
    A4 = A3 + 32768
    masks = bf(A4, A4 + 9216)
    A5 = A4 + 9216
    av_f = f32(A5, A5 + 2048)
    avn_f = f32(A5 + 2048, A5 + 4096)
    av_h = bf(A5 + 4096, A5 + 8448)
    av_l = bf(A5 + 8448, A5 + 12800)
    A6 = A5 + 12800
    vstage = f32(A6, A6 + 4096)
    vsth = bf(A6 + 4096, A6 + 6144)
    vstl = bf(A6 + 6144, A6 + 8192)
    kqstage = f32(A6 + 8192, A6 + 10240)
    ef32[2] = f32(A6, A6 + 4096)
    ehl[2] = bf(A6 + 4096, A6 + 8192)
    ATT_END = A6 + 10240
    assert ATT_END <= POOLB

    x2_res = f32(0, 69632)
    O1 = 69632
    oav_h = [bf(O1 + i * 4352, O1 + (i + 1) * 4352) for i in range(2)]
    oav_l = [bf(O1 + 8704 + i * 4352, O1 + 8704 + (i + 1) * 4352) for i in range(2)]
    O2 = O1 + 17408
    owo = [[bf(O2 + (2 * i + l) * 2048, O2 + (2 * i + l + 1) * 2048)
            for l in range(2)] for i in range(2)]
    O3 = O2 + 8192
    x1st = [f32(O3 + i * 4096, O3 + (i + 1) * 4096) for i in range(2)]
    O_END = O3 + 8192
    assert O_END <= POOLB

    def barrier():
        vals = [(s, s.v) for s in ([sV, sC, sT, sG, sLD, sSP, sTR, sWST, sX,
                 sWQK, sWVB, sMSK, sOAV, sTRid, sTRlt, sTRg, sTRe, sTRq, sTRt]
                + sWS + sW1 + sW2 + sXS + sVBf + sX1)]
        for eng in ENGS:
            for s, v in vals:
                p.wait(eng, s, v)

    def split(src, hi, lo):
        p.op("vector", sV, "tensor_copy", hi, src)
        p.op("vector", sV, "tensor_tensor", lo, src, hi, OP.subtract)
        return sV.v

    ALLSEMS = ([sLD, sSP, sTR, sV, sC, sT, sG, sWST, sX, sWQK, sWVB, sMSK,
                sOAV, sTRid, sTRlt, sTRg, sTRe, sTRq, sTRt]
               + sWS + sW1 + sW2 + sXS + sVBf + sX1)
    ckpts = {}
    p.op("vector", sV, "memset", c_eps[:], 1e-5)
    p.op("vector", sV, "memset", c_expb[:], -EXPC)
    p.op("vector", sV, "memset", c_ones1[:], 1.0)
    p.op("vector", sV, "memset", c_onesf[:], 1.0)
    p.op("vector", sV, "memset", c_onesb[:], 1.0)

    p.dma("sync", sLD, identb[:], d_id[:])
    p.dma("sync", sLD, b1p[:], d_b1[:])
    p.dma("sync", sLD, qbp[:], d_qb[:])
    p.dma("sync", sLD, kbp[:], d_kb[:])
    p.dma("sync", sLD, screp[:], d_sc[:])
    p.dma("sync", sLD, b2f[0:1, 0:DM], d_b2[:])
    p.dma("sync", sLD, obf[0:1, 0:DM], d_ob[:])
    t = p.dma("sync", sLD, vbf[0:1, 0:DM], d_vb[:])
    CONSTS_LD = sLD.v
    p.wait("vector", sLD, CONSTS_LD)
    p.op("vector", sV, "tensor_copy", identf0[:], identb[0:16, 0:16])
    p.wait("gpsimd", sV, sV.v)
    p.dma("gpsimd", sTRid, identf[:], identf0[:])
    for rf, rbp in ((b2f, b2b), (obf, obb), (vbf, vbb)):
        split(rf[0:1, 0:DM], rbp[0:1, 0:DM], rbp[0:1, DM:2 * DM])

    wsf = [f32(R2, R2 + 16384), f32(R2 + 16384, R2 + 32768)]
    wsh = bf(R2 + 32768, R2 + 40960)
    wsl = bf(R2 + 40960, R2 + 49152)
    jobs = []
    w1v = d_w1t.rearrange("(dc p) h -> dc p h", p=128)
    w1s0 = d_w1s[0].rearrange("(dc p) h -> dc p h", p=128)
    w1s1 = d_w1s[1].rearrange("(dc p) h -> dc p h", p=128)
    for i in range(DC):
        jobs.append((w1v[i], w1s0[i], w1s1[i], HF))
    w2v = d_w2t.rearrange("(hc p) d -> hc p d", p=128)
    w2s0 = d_w2s[0].rearrange("(hc p) d -> hc p d", p=128)
    w2s1 = d_w2s[1].rearrange("(hc p) d -> hc p d", p=128)
    for i in range(HC):
        jobs.append((w2v[i], w2s0[i], w2s1[i], DM))
    for dsrc, ddst in ((d_wq, d_wqs), (d_wk, d_wks), (d_wv, d_wvs), (d_wo, d_wos)):
        sv_ = dsrc.rearrange("(dc p) c -> dc p c", p=128)
        dv0 = ddst[0].rearrange("(dc p) c -> dc p c", p=128)
        dv1 = ddst[1].rearrange("(dc p) c -> dc p c", p=128)
        for i in range(DC):
            jobs.append((sv_[i], dv0[i], dv1[i], DM))
    wv_done, wst_done = [], []
    CKJOBS = DBG.get("njobs", len(jobs))
    jobs = jobs[:CKJOBS]
    for i, (src, dsth, dstl, width) in enumerate(jobs):
        buf = wsf[i % 2]
        if i >= 2:
            p.wait("sync", sV, wv_done[i - 2])
        t = p.dma("sync", sWS[i % 2], buf[:, 0:width], src)
        p.wait("vector", sWS[i % 2], t)
        if i >= 1:
            p.wait("vector", sWST, wst_done[i - 1])
        p.op("vector", sV, "tensor_copy", wsh[:, 0:width], buf[:, 0:width])
        p.op("vector", sV, "tensor_tensor", wsl[:, 0:width], buf[:, 0:width],
             wsh[:, 0:width], OP.subtract)
        wv_done.append(sV.v)
        p.wait("scalar", sV, sV.v)
        p.dma("scalar", sWST, dsth, wsh[:, 0:width])
        wst_done.append(p.dma("scalar", sWST, dstl, wsl[:, 0:width]))

    def emit_ln_tile(x_tile, localslot, transpose, ncols):
        p.wait("vector", sSP, sSP.v)
        ss, qq, mm, vv, rr = (lnsm[:, 0:1], lnsm[:, 1:2], lnsm[:, 2:3],
                              lnsm[:, 3:4], lnsm[:, 4:5])
        p.op("vector", sV, "reduce_sum", ss, x_tile, axis=AX.X)
        p.op("scalar", sC, "activation", sq_scr[:, 0:DM], x_tile, AF.Square,
             accum_out=qq)
        csq = sC.v
        p.wait("vector", sV, sV.v)
        p.op("vector", sV, "tensor_scalar_mul", mm, ss, 1.0 / DM)
        p.op("vector", sV, "tensor_mul", vv, mm, mm)
        p.wait("vector", sC, csq)
        p.op("vector", sV, "tensor_scalar", vv, qq, 1.0 / DM, vv[:, 0:1],
             OP.mult, OP.subtract)
        p.wait("scalar", sV, sV.v)
        p.op("scalar", sC, "activation", rr, vv, AF.Sqrt, bias=c_eps[:, 0:1],
             scale=1.0)
        p.wait("vector", sC, sC.v)
        p.op("vector", sV, "reciprocal", rr, rr)
        p.wait("vector", sV, sV.v)
        p.op("vector", sV, "tensor_scalar", ln_stage[:, 0:DM], x_tile, mm[:, 0:1],
             rr[:, 0:1], OP.subtract, OP.mult)
        split(ln_stage[:, 0:DM], ln_hi[:, 0:DM], ln_lo[:, 0:DM])
        vs = sV.v
        if transpose:
            p.wait("scalar", sV, vs)
            for dc in range(DC):
                p.dmat("scalar", sSP,
                       xn_h[:, dc * ncols + localslot * 128:
                            dc * ncols + localslot * 128 + 128],
                       ln_hi[:, dc * 128:dc * 128 + 128])
                p.dmat("scalar", sSP,
                       xn_l[:, dc * ncols + localslot * 128:
                            dc * ncols + localslot * 128 + 128],
                       ln_lo[:, dc * 128:dc * 128 + 128])
        return vs

    def emit_lora_prep():
        p.wait("gpsimd", sV, sV.v)
        t = p.dma("gpsimd", sTRlt, lt_tm[0:16, 0:DM], ln_stage[1:17, 0:DM])
        p.wait("tensor", sTRlt, t)
        p.wait("tensor", sTRid, 16)
        for dc in range(DC):
            p.op("tensor", sT, "transpose", PS[3][0:128, 512:528],
                 lt_tm[0:16, dc * 128:dc * 128 + 128], identf[:])
            p.wait("vector", sT, sT.v)
            p.op("vector", sV, "tensor_copy", lt_fm[:, dc * 16:dc * 16 + 16],
                 PS[3][0:128, 512:528])
            p.wait("tensor", sV, sV.v)
        split(lt_fm[:], lt_fm_h[:], lt_fm_l[:])
        p.op("vector", sV, "memset", e2d[:], 0.0)
        for i in range(4):
            p.wait("gpsimd", sV, sV.v)
            t_ = p.dma("gpsimd", sTRg, gstage[0:1, 0:DM],
                       lt_tm[8 + 2 * i:8 + 2 * i + 1, 0:DM])
            p.wait("vector", sTRg, t_)
            p.wait("vector", sTRe, sTRe.v)
            p.op("vector", sV, "tensor_copy",
                 gq[0:1, 0:DM].rearrange("p (q r) -> p q r", q=4),
                 gstage[0:1, 0:DM].rearrange("p (r q) -> p q r", q=4))
            p.wait("gpsimd", sV, sV.v)
            p.dma("gpsimd", sTRe,
                  e2d[4 * i:4 * i + 4, i * 256:(i + 1) * 256],
                  gq[0:1, 0:DM].rearrange("p (q r) -> p q r", q=4))
        p.wait("vector", sTRe, sTRe.v)
        split(e2d[:], e2h[:], e2l[:])
        return sV.v

    def emit_ffn_half(slots, out_phase):
        ncols = len(slots) * 128
        if ncols == 1152:
            grps = [(0, 128), (128, 512), (640, 512)]
        else:
            assert ncols == 1024
            grps = [(0, 512), (512, 512)]
        lora_v = sV.v
        for (o, w) in grps:
            p.wait("tensor", sV, lora_v)
            n3 = DC * 3
            k = 0
            for dc in range(DC):
                for pi, (lh, rh) in enumerate(((lt_fm_h, xn_h), (lt_fm_h, xn_l),
                                               (lt_fm_l, xn_h))):
                    k += 1
                    p.op("tensor", sT, "matmul", PS[3][64:68, 512:512 + w],
                         lh[:, dc * 16 + 1:dc * 16 + 9:2],
                         rh[:, dc * ncols + o:dc * ncols + o + w],
                         start=(k == 1), stop=(k == n3),
                         tile_position=(0, 64))
            p.wait("vector", sT, sT.v)
            p.op("vector", sV, "tensor_scalar", p4f[0:4, o:o + w],
                 PS[3][64:68, 512:512 + w], screp[64:68, 0:1], None, OP.mult)
            p.wait("tensor", sV, sV.v)
        split(p4f[0:4, 0:ncols], p4h[0:4, 0:ncols], p4l[0:4, 0:ncols])
        p.op("vector", sV, "memset", t4f[0:4, 0:4 * ncols], 0.0)
        vprep = sV.v
        p.wait("tensor", sV, vprep)

        m1ps = [(PS[0], 0), (PS[0], 512)]
        m1_gelu_done = {}
        m1ctr = 0
        w1_ld, w2_ld = {}, {}
        w1use, w2use = {}, {}
        w1r = d_w1s.rearrange("l (dc p) h -> l p dc h", p=128)
        w2r = d_w2s.rearrange("l (hc p) d -> l hc p d", p=128)

        def load_w1(hc):
            for lvl in range(2):
                slot = (2 * hc + lvl) % 4
                if slot in w1use:
                    p.wait("sync", sT, w1use[slot])
                p.dma("sync", sW1[slot],
                      w1b[slot][:].rearrange("p (dc c) -> p dc c", dc=DC),
                      w1r[lvl][:, :, hc * 128:hc * 128 + 128])
                w1_ld[(hc, lvl)] = (slot, sW1[slot].v)

        def load_w2(hc):
            for lvl in range(2):
                slot = (2 * hc + lvl) % 8
                if slot in w2use:
                    p.wait("sync", sT, w2use[slot])
                p.dma("sync", sW2[slot], w2b[slot][:], w2r[lvl][hc])
                w2_ld[(hc, lvl)] = (slot, sW2[slot].v)

        load_w1(0)
        load_w1(1)
        for hc0 in range(4):
            load_w2(hc0)
        m2ps = [PS[2], PS[3]]
        m2_last = {}
        t_evac_last = {}
        tctr = [0]

        e1q_v = 0
        NBLK = DBG.get('nblk', 8)
        for blk in range(NBLK):
            if blk % 2 == 0:
                qt = blk // 2
                p.wait("vector", sT, sT.v)
                p.op("vector", sV, "memset", e1qf[:], 0.0)
                p.wait("gpsimd", sV, sV.v)
                t_ = p.dma("gpsimd", sTRq, e1qf[qt:qt + 1, 0:DM],
                           lt_tm[2 * qt:2 * qt + 1, 0:DM])
                p.wait("vector", sTRq, t_)
                split(e1qf[0:4, 0:DM], e1qh[0:4, 0:DM], e1ql[0:4, 0:DM])
                e1q_v = sV.v
            for ci in range(4):
                hc = blk * 4 + ci
                s1h, t1h = w1_ld[(hc, 0)]
                s1l, t1l = w1_ld[(hc, 1)]
                p.wait("tensor", sW1[s1h], t1h)
                p.wait("tensor", sW1[s1l], t1l)
                for (o, w) in grps:
                    ps, po = m1ps[m1ctr % 2]
                    if m1ctr - 2 in m1_gelu_done:
                        p.wait("tensor", sC, m1_gelu_done[m1ctr - 2])
                    nmm = 3 * DC + (0 if DBG.get("no_m1lora") else 3)
                    k = 0
                    for pi, (wsl_, rh) in enumerate(((s1h, xn_h), (s1h, xn_l),
                                                     (s1l, xn_h))):
                        for dc in range(DC):
                            k += 1
                            p.op("tensor", sT, "matmul", ps[0:128, po:po + w],
                                 w1b[wsl_][:, dc * 128:dc * 128 + 128],
                                 rh[:, dc * ncols + o:dc * ncols + o + w],
                                 start=(k == 1), stop=(k == nmm))
                    p.wait("tensor", sV, e1q_v)
                    if DBG.get("no_m1lora"):
                        nmm = 3 * DC
                    for pi, (lh, rh) in enumerate(((e1qh, p4h), (e1qh, p4l),
                                                   (e1ql, p4h))):
                        if DBG.get("no_m1lora"):
                            break
                        k += 1
                        p.op("tensor", sT, "matmul", ps[0:128, po:po + w],
                             lh[0:4, (hc % 8) * 128:(hc % 8) * 128 + 128],
                             rh[0:4, o:o + w], start=False, stop=(k == nmm),
                             tile_position=(64, 0))
                    tmm = sT.v
                    w1use[s1h] = tmm
                    w1use[s1l] = tmm
                    p.wait("scalar", sT, tmm)
                    p.wait("scalar", sV, sV.v)
                    p.op("scalar", sC, "activation", gel[:, 0:w],
                         ps[0:128, po:po + w], AF.Gelu,
                         bias=b1p[:, hc:hc + 1], scale=1.0)
                    m1_gelu_done[m1ctr] = sC.v
                    m1ctr += 1
                    p.wait("vector", sC, sC.v)
                    split(gel[:, 0:w],
                          h_h[:, ci * ncols + o:ci * ncols + o + w],
                          h_l[:, ci * ncols + o:ci * ncols + o + w])
                hs = sV.v
                quarter = hc // 8
                dcl = hc % 8
                p.wait("tensor", sV, hs)
                for (o, w) in grps:
                    tslot = tctr[0] % 2
                    tctr[0] += 1
                    if tslot in t_evac_last:
                        p.wait("tensor", sV, t_evac_last[tslot])
                    for pi, (lh, rh) in enumerate(((lt_fm_h, h_h), (lt_fm_h, h_l),
                                                   (lt_fm_l, h_h))):
                        p.op("tensor", sT, "matmul",
                             PS[1][0:4, tslot * 512:tslot * 512 + w],
                             lh[:, dcl * 16 + 9:dcl * 16 + 16:2],
                             rh[:, ci * ncols + o:ci * ncols + o + w],
                             start=(pi == 0), stop=(pi == 2))
                    p.wait("vector", sT, sT.v)
                    p.op("vector", sV, "scalar_tensor_tensor",
                         t4f[0:4, quarter * ncols + o:quarter * ncols + o + w],
                         PS[1][0:4, tslot * 512:tslot * 512 + w],
                         screp[0:4, 0:1],
                         t4f[0:4, quarter * ncols + o:quarter * ncols + o + w],
                         OP.mult, OP.add)
                    t_evac_last[tslot] = sV.v
                if hc + 2 < HC:
                    load_w1(hc + 2)
            for ti, sl in enumerate(slots):
                ps = m2ps[ti % 2]
                if ti % 2 in m2_last:
                    p.wait("tensor", sV, m2_last[ti % 2])
                nmm = 4 * 3 * 2
                k = 0
                for ci in range(4):
                    hc = blk * 4 + ci
                    s2h, t2h = w2_ld[(hc, 0)]
                    s2l, t2l = w2_ld[(hc, 1)]
                    if ti == 0:
                        p.wait("tensor", sW2[s2h], t2h)
                        p.wait("tensor", sW2[s2l], t2l)
                    for pi, (hh, wsl_) in enumerate(((h_h, s2h), (h_l, s2h),
                                                     (h_h, s2l))):
                        for half in range(2):
                            k += 1
                            p.op("tensor", sT, "matmul",
                                 ps[0:128, half * 512:half * 512 + 512],
                                 hh[:, ci * ncols + ti * 128:
                                    ci * ncols + ti * 128 + 128],
                                 w2b[wsl_][:, half * 512:half * 512 + 512],
                                 start=(ci == 0 and pi == 0),
                                 stop=(k in (nmm - 1, nmm)))
                    w2use[s2h] = sT.v
                    w2use[s2l] = sT.v
                tmm = sT.v
                p.wait("vector", sT, tmm)
                p.op("vector", sV, "scalar_tensor_tensor",
                     x_res[:, sl * DM:(sl + 1) * DM], ps[0:128, 0:1024], 0.5,
                     x_res[:, sl * DM:(sl + 1) * DM], OP.mult, OP.add)
                m2_last[ti % 2] = sV.v
            if blk + 1 < NBLK:
                for hc2 in range(4 * blk + 4, 4 * blk + 8):
                    load_w2(hc2)

        if DBG.get("no_m2x"):
            return
        NO_T16 = DBG.get("no_t16")
        NO_B2K = DBG.get("no_b2k")
        if not NO_T16:
            p.wait("gpsimd", sV, sV.v)
            for i in range(4):
                for q_ in range(4):
                    p.dma("gpsimd", sTRt,
                          t16f[4 * i + q_:4 * i + q_ + 1, 0:ncols],
                          t4f[i:i + 1, q_ * ncols:(q_ + 1) * ncols])
            p.wait("vector", sTRt, sTRt.v)
            split(t16f[0:16, 0:ncols], t16h[0:16, 0:ncols], t16l[0:16, 0:ncols])
        p.wait("tensor", sV, sV.v)
        for ti, sl in enumerate(slots):
            ps = m2ps[ti % 2]
            if ti % 2 in m2_last:
                p.wait("tensor", sV, m2_last[ti % 2])
            for half in range(2):
                k = 0
                nmm2 = (0 if NO_T16 else 3) + (0 if NO_B2K else 2)
                for pi, (th, eh) in enumerate(((t16h, e2h), (t16h, e2l),
                                               (t16l, e2h))):
                    if NO_T16:
                        break
                    k += 1
                    p.op("tensor", sT, "matmul",
                         ps[0:128, half * 512:half * 512 + 512],
                         th[0:16, ti * 128:ti * 128 + 128],
                         eh[0:16, half * 512:half * 512 + 512],
                         start=(k == 1), stop=(k == nmm2),
                         tile_position=(32, 0))
                for lvl in range(2):
                    if NO_B2K:
                        break
                    k += 1
                    p.op("tensor", sT, "matmul",
                         ps[0:128, half * 512:half * 512 + 512],
                         c_onesb[32:33, 0:128],
                         b2b[0:1, lvl * DM + half * 512:
                             lvl * DM + half * 512 + 512],
                         start=(k == 1), stop=(k == nmm2),
                         tile_position=(32, 0))
            tmm = sT.v
            p.wait("vector", sT, tmm)
            p.op("vector", sV, "scalar_tensor_tensor",
                 x_res[:, sl * DM:(sl + 1) * DM], ps[0:128, 0:1024], 0.5,
                 x_res[:, sl * DM:(sl + 1) * DM], OP.mult, OP.add)
            m2_last[ti % 2] = sV.v
            if out_phase:
                p.wait("scalar", sV, sV.v)
                p.dma("scalar", sSP, d_out[sl - 1],
                      x_res[:, sl * DM:(sl + 1) * DM])

    HALVES1 = [list(range(0, 9)), list(range(9, 17))]

    def emit_ln_ffn(which):
        out_phase = (which == 2)
        for hi_, slots in enumerate(HALVES1):
            if ckpts is not None:
                ckpts[f"pre{which}{hi_}"] = p.checkpoint(ALLSEMS)
            ffn_slots = slots if which == 1 else [s for s in slots if s != 0]
            ncols = len(ffn_slots) * 128
            barrier()
            for li, sl in enumerate(slots):
                do_tr = sl in ffn_slots
                lsl = ffn_slots.index(sl) if do_tr else 0
                emit_ln_tile(x_res[:, sl * DM:(sl + 1) * DM], lsl, do_tr, ncols)
                if sl == 0:
                    emit_lora_prep()
            if ckpts is not None:
                ckpts[f"ln{which}{hi_}"] = p.checkpoint(ALLSEMS)
            p.wait("tensor", sSP, sSP.v)
            emit_ffn_half(ffn_slots, out_phase)
            if ckpts is not None:
                ckpts[f"ffnh{which}{hi_}"] = p.checkpoint(ALLSEMS)

    xld = []
    for tt in range(NT):
        xld.append(p.dma("sync", sX, x_res[:, tt * DM:(tt + 1) * DM], d_x[tt]))
    p.wait("vector", sX, sX.v)
    p.wait("vector", sWST, sWST.v)
    ckpts["wsplit"] = p.checkpoint(ALLSEMS)
    emit_ln_ffn(1)

    barrier()
    p.dma("scalar", sSP, d_x1sp[:].rearrange("t p d -> p t d"),
          x_res[:].rearrange("p (t d) -> p t d", t=NT))
    ckpts["ffn1"] = p.checkpoint(ALLSEMS)
    for hi_, slots in enumerate(HALVES1):
        ncols = len(slots) * 128
        barrier()
        for li, sl in enumerate(slots):
            emit_ln_tile(x_res[:, sl * DM:(sl + 1) * DM], li, True, ncols)

        base = slots[0]
        p.wait("scalar", sSP, sSP.v)
        for dc in range(DC):
            for lvl, src in ((0, xn_h), (1, xn_l)):
                p.dma("scalar", sSP,
                      d_xnown[lvl][dc][:, base * 128:base * 128 + ncols],
                      src[:, dc * ncols:dc * ncols + ncols])
                lo = 1 if base == 0 else base
                off = (lo - base) * 128
                p.dma("scalar", sSP,
                      d_agin[lvl][dc][:, (lo - 1) * 128:(lo - 1) * 128 + ncols - off],
                      src[:, dc * ncols + off:dc * ncols + ncols])
    agin_done = sSP.v
    p.wait("gpsimd", sSP, agin_done)
    sAG.v += 1

    RGROUPS = [[0, 1]] if DBG.get("rg2") else [[0, 1], [2, 3], [4, 5], [6, 7]]

    def fn_ag(q, h=sAG.h):
        q.collective_compute(
            "AllGather", OP.bypass,
            replica_groups=RGROUPS,
            ins=[d_agin.opt()], outs=[d_agout.opt()],
        ).then_inc(h, 1)
    p.emit("gpsimd", fn_ag)
    p.wait("sync", sAG, 1)
    if DBG.get("dump_ag"):
        d_agdbg = nc.dram_tensor("agdbg", [2, 2, DC, 128, 2048], BF16,
                                 kind="ExternalOutput").ap()
        p.dma("sync", sLD, d_agdbg[:], d_agout[:])
    ckpts["ag"] = p.checkpoint(ALLSEMS)

    barrier()

    t = p.dma("sync", sMSK, masks[:].rearrange("p (m c) -> p m c", m=9),
              d_msk[:].rearrange("m p c -> p m c"))
    p.wait("tensor", sMSK, 16)
    wqr = d_wqs.rearrange("l (dc p) c -> l p dc c", p=128)
    wkr = d_wks.rearrange("l (dc p) c -> l p dc c", p=128)
    wvr = d_wvs.rearrange("l (dc p) c -> l p dc c", p=128)
    p.wait("tensor", sAG, 1)
    p.wait("sync", sAG, 1)

    for lvl in range(2):
        p.dma("sync", sWVB, wvb[lvl][:].rearrange("p (dc c) -> p dc c", dc=DC),
              wvr[lvl])
    p.wait("tensor", sWVB, 32)
    agv = d_agout.rearrange("r l dc p c -> r l dc p c")
    xs_ld = {}

    def load_xs(tb, bufi):
        rank, off = tb // 4, (tb % 4) * 512
        for lvl in range(2):
            p.dma(
                "sync", sXS[bufi],
                xs_b[bufi][lvl][:].rearrange("p (dc c) -> p dc c", dc=DC),
                d_agout[rank][lvl][:, :, off:off + 512].rearrange(
                    "dc p c -> p dc c"))
            xs_ld[(tb, lvl)] = (bufi, sXS[bufi].v)

    load_xs(0, 0)
    load_xs(1, 1)
    vwld = sLD.v
    vps = [PS[0], PS[1]]
    vevac = []
    for tb in range(8):
        bufi = tb % 2
        p.wait("tensor", sXS[bufi], xs_ld[(tb, 1)][1])
        for g4 in range(4):
            g = tb * 4 + g4
            ps = vps[g % 2]
            if len(vevac) >= 2:
                p.wait("tensor", sV, vevac[-2])
            for half in range(2):
                k = 0
                for pi, (xl, wl) in enumerate(((0, 0), (1, 0), (0, 1))):
                    for dc in range(DC):
                        k += 1
                        p.op("tensor", sT, "matmul",
                             ps[0:128, half * 512:half * 512 + 512],
                             xs_b[bufi][xl][:, dc * 512 + g4 * 128:
                                            dc * 512 + g4 * 128 + 128],
                             wvb[wl][:, dc * 1024 + half * 512:
                                     dc * 1024 + half * 512 + 512],
                             start=(k == 1), stop=False)
                for lvl in range(2):
                    p.op("tensor", sT, "matmul",
                         ps[0:128, half * 512:half * 512 + 512],
                         c_onesb[64:65, 0:128],
                         vbb[0:1, lvl * DM + half * 512:
                             lvl * DM + half * 512 + 512],
                         start=False, stop=(lvl == 1), tile_position=(64, 0))
            tmm = sT.v
            p.wait("vector", sT, tmm)
            p.wait("vector", sSP, sSP.v)
            p.op("vector", sV, "tensor_copy", vstage[:, 0:1024], ps[0:128, 0:1024])
            split(vstage[:, 0:1024], vsth[:, 0:1024], vstl[:, 0:1024])
            vevac.append(sV.v)
            p.wait("scalar", sV, sV.v)
            p.dma("scalar", sSP, d_vtm[0][g], vsth[:, 0:1024])
            p.dma("scalar", sSP, d_vtm[1][g], vstl[:, 0:1024])
        if tb + 2 < 8:
            p.wait("sync", sT, sT.v)
            load_xs(tb + 2, bufi)
    vspill_done = sSP.v
    ckpts["vproj"] = p.checkpoint(ALLSEMS)

    QGRPS = [(0, 128), (128, 512), (640, 512), (1152, 512), (1664, 512)]

    def klist_for(j):
        ks = []
        for g in range(4 * j + 4):
            ks.append((g, (g - 4 * j) if g >= 4 * j else None))
        for g in range(4 * j + 4):
            ks.append((16 + g, (4 + g - 4 * j) if g >= 4 * j else None))
        return ks

    vfetch_ranges = {
        "sh": [(0, 1)],
        0: [(1, 4), (16, 20)],
        1: [(4, 8), (20, 24)],
        2: [(8, 12), (24, 28)],
        3: [(12, 16), (28, 32)],
    }
    av_last_read = 0
    kq_last_use = 0
    vb_last_use = 0
    p.wait("sync", sSP, vspill_done)
    p.wait("gpsimd", sSP, vspill_done)
    p.wait("vector", sSP, vspill_done)
    for pi_ in range(NPAIR):
        wq_ld = []
        for mi, wsrc in ((0, wqr), (1, wkr)):
            for lvl in range(2):
                wq_ld.append(p.dma(
                    "sync", sWQK,
                    wqk[mi][lvl][:].rearrange("p (dc c) -> p dc c", dc=DC),
                    wsrc[lvl][:, :, pi_ * 128:pi_ * 128 + 128]))
        kq_last = {}
        mm_done = []

        def load_q(gi, bufi):
            o, w = QGRPS[gi]
            for lvl in range(2):
                p.dma(
                    "sync", sXS[bufi],
                    xs_b[bufi][lvl][:, 0:DC * 512].rearrange(
                        "p (dc c) -> p dc c", dc=DC)[:, :, 0:w],
                    d_xnown[lvl][:, :, o:o + w].rearrange("dc p c -> p dc c"))

        seq = [("k", tb) for tb in range(8)] + [("q", gi) for gi in range(5)]
        p.wait("sync", sT, sT.v)
        if seq[0][0] == "k":
            load_xs(0, 0)
        for si, (kind, idx) in enumerate(seq):
            bufi = si % 2
            if si + 1 < len(seq):
                if si >= 1:
                    p.wait("sync", sT, mm_done[si - 1])
                nk_, ni_ = seq[si + 1]
                if nk_ == "k":
                    load_xs(ni_, (si + 1) % 2)
                else:
                    load_q(ni_, (si + 1) % 2)
            p.wait("tensor", sXS[bufi], sXS[bufi].v if kind == "q" or idx > 0
                   else xs_ld[(0, 1)][1])
            p.wait("tensor", sWQK, wq_ld[-1])
            if kind == "k":
                o, w, mi = idx * 512, 512, 1
                ps = PS[2][0:128, (idx % 2) * 512:(idx % 2) * 512 + 512]
            else:
                o, w = QGRPS[idx]
                mi = 0
                ps = PS[3][0:128, (idx % 2) * 512:(idx % 2) * 512 + w]
            if (kind, idx % 2) in kq_last:
                p.wait("tensor", sV, kq_last[(kind, idx % 2)])
            k = 0
            for xl, wl in ((0, 0), (1, 0), (0, 1)):
                for dc in range(DC):
                    k += 1
                    p.op("tensor", sT, "matmul", ps,
                         wqk[mi][wl][:, dc * 128:dc * 128 + 128],
                         xs_b[bufi][xl][:, dc * 512:dc * 512 + w],
                         start=(k == 1), stop=(k == 24))
            mm_done.append(sT.v)
            p.wait("vector", sT, sT.v)
            bias = kbp if kind == "k" else qbp
            p.op("vector", sV, "tensor_scalar", kqstage[:, 0:w], ps,
                 bias[:, pi_:pi_ + 1], None, OP.add)
            if kind == "k":
                split(kqstage[:, 0:512], kf_h[:, o:o + 512],
                      kf_l[:, o:o + 512])
            else:
                split(kqstage[:, 0:w], q_h[:, o:o + w], q_l[:, o:o + w])
            kq_last[(kind, idx % 2)] = sV.v
        kq_done_v = sV.v

        p.wait("tensor", sV, kq_done_v)
        p.wait("tensor", sV, av_last_read)
        fetched = set()
        qblocks = [("sh", 0, 128, [(0, 8)])]
        for j in range(4):
            qblocks.append((j, (4 * j + 1) * 128, 512, klist_for(j)))
        av_read_v = av_last_read
        exp_hist = {}
        eready_v = {}
        eready_g = {}
        avsum_hist = {}
        for qi, (qb, qo, qw, klist) in enumerate(qblocks):
            p.wait("gpsimd", sT, sT.v)
            p.wait("vector", sT, sT.v)
            p.wait("sync", sT, vb_last_use)
            for (g0, g1) in vfetch_ranges[qb]:
                for lvl, dst in ((0, vb_h), (1, vb_l)):
                    p.dma("sync", sVBf[qi % 2], dst[:, g0 * 128:g1 * 128].rearrange(
                        "p (g c) -> p g c", g=g1 - g0),
                        d_vtm[lvl][g0:g1, :, pi_ * 128:pi_ * 128 + 128].rearrange(
                            "g p c -> p g c"))
            p.wait("tensor", sVBf[qi % 2], sVBf[qi % 2].v)
            nk = len(klist)
            LAG = 2

            def emit_av(tt):
                g2, _m = klist[tt]
                p.wait("tensor", sG, eready_g[tt])
                p.wait("tensor", sV, eready_v[tt])
                if tt == 0:
                    p.wait("tensor", sV, av_read_v)
                for hd in range(2):
                    k = 0
                    avreg = (PS[2][0:64, 0:qw] if hd == 0 else
                             PS[2][64:128, 512:512 + qw])
                    for vx, eoff in ((vb_h, 0), (vb_l, 0), (vb_h, 1024)):
                        k += 1
                        p.op("tensor", sT, "matmul", avreg,
                             vx[:, g2 * 128 + hd * 64:g2 * 128 + hd * 64 + 64],
                             ehl[tt % 3][:, eoff + hd * 512:
                                         eoff + hd * 512 + qw],
                             start=(tt == 0 and k == 1),
                             stop=(tt == nk - 1 and k == 3),
                             tile_position=(0, hd * 64))
                    for li_, eoff in ((0, 0), (1, 1024)):
                        p.op("tensor", sT, "matmul",
                             PS[3][hd * 64:hd * 64 + 1, 0:qw],
                             c_ones1[:, 0:1],
                             ehl[tt % 3][:, eoff + hd * 512:
                                         eoff + hd * 512 + qw],
                             start=(tt == 0 and li_ == 0),
                             stop=(tt == nk - 1 and li_ == 1),
                             tile_position=(0, hd * 64))
                avsum_hist[tt] = sT.v

            for t, (g, mslot) in enumerate(klist):
                grp = PS[t % 2]
                if t - 2 in exp_hist:
                    p.wait("tensor", sC, exp_hist[t - 2])
                elif t < 2 and qi > 0:
                    p.wait("tensor", sC, sC.v)
                for hd in range(2):
                    reg = grp[0:128, hd * 512:hd * 512 + qw]
                    k = 0
                    for kx, qx in ((kf_h, q_h), (kf_h, q_l), (kf_l, q_h)):
                        k += 1
                        p.op("tensor", sT, "matmul", reg,
                             kx[hd * 64:hd * 64 + 64, g * 128:g * 128 + 128],
                             qx[hd * 64:hd * 64 + 64, qo:qo + qw],
                             start=(k == 1), stop=(k == 3 and mslot is None),
                             tile_position=(hd * 64, 0))
                    if mslot is not None:
                        p.op("tensor", sT, "matmul", reg,
                             identb[:],
                             masks[:, mslot * 512:mslot * 512 + qw],
                             start=False, stop=True)
                tqk = sT.v
                if t - LAG >= 0:
                    emit_av(t - LAG)
                p.wait("scalar", sT, tqk)
                if t - 3 in eready_v:
                    p.wait("scalar", sV, eready_v[t - 3])
                    p.wait("scalar", sG, eready_g[t - 3])
                for hd in range(2):
                    p.op("scalar", sC, "activation",
                         ef32[t % 3][:, hd * 512:hd * 512 + qw],
                         grp[0:128, hd * 512:hd * 512 + qw], AF.Exp,
                         bias=c_expb[:, 0:1], scale=0.125)
                exp_hist[t] = sC.v
                p.wait("gpsimd", sC, exp_hist[t])
                if t - 3 in avsum_hist:
                    p.wait("gpsimd", sT, avsum_hist[t - 3])
                    p.wait("vector", sT, avsum_hist[t - 3])
                for hd in range(2):
                    p.op("gpsimd", sG, "tensor_copy",
                         ehl[t % 3][:, hd * 512:hd * 512 + qw],
                         ef32[t % 3][:, hd * 512:hd * 512 + qw])
                eready_g[t] = sG.v
                p.wait("vector", sC, exp_hist[t])
                p.wait("vector", sG, eready_g[t])
                for hd in range(2):
                    p.op("vector", sV, "tensor_tensor",
                         ehl[t % 3][:, 1024 + hd * 512:1024 + hd * 512 + qw],
                         ef32[t % 3][:, hd * 512:hd * 512 + qw],
                         ehl[t % 3][:, hd * 512:hd * 512 + qw], OP.subtract)
                eready_v[t] = sV.v
            for tt in range(max(0, nk - LAG), nk):
                emit_av(tt)
            vb_last_use = sT.v
            core_done = sT.v
            p.wait("vector", sT, core_done)
            p.op("vector", sV, "tensor_copy", av_f[0:64, 0:qw], PS[2][0:64, 0:qw])
            p.op("vector", sV, "tensor_copy", av_f[64:128, 0:qw],
                 PS[2][64:128, 512:512 + qw])
            p.op("vector", sV, "tensor_copy", srow[0:1, 0:qw], PS[3][0:1, 0:qw])
            p.op("vector", sV, "tensor_copy", srow[64:65, 0:qw], PS[3][64:65, 0:qw])
            p.op("vector", sV, "reciprocal", srow[0:1, 0:qw], srow[0:1, 0:qw])
            p.op("vector", sV, "reciprocal", srow[64:65, 0:qw], srow[64:65, 0:qw])
            vrec = sV.v
            p.wait("tensor", sV, vrec)
            for hd in range(2):
                p.op("tensor", sT, "matmul", PS[3][hd * 64:hd * 64 + 64,
                                                   512:512 + qw],
                     c_onesf[hd * 64:hd * 64 + 1, 0:64],
                     srow[hd * 64:hd * 64 + 1, 0:qw],
                     start=True, stop=True, tile_position=(hd * 64, hd * 64))
            trep = sT.v
            p.wait("vector", sT, trep)
            p.op("vector", sV, "tensor_tensor", avn_f[:, 0:qw], av_f[:, 0:qw],
                 PS[3][0:128, 512:512 + qw], OP.mult)
            if qi == 0:
                p.wait("vector", sSP, sSP.v)
            split(avn_f[:, 0:qw], av_h[:, qo:qo + qw], av_l[:, qo:qo + qw])
            av_read_v = sV.v
        av_last_read = av_read_v
        p.wait("scalar", sV, av_read_v)
        p.dma("scalar", sSP, d_avsp[0][pi_], av_h[:])
        p.dma("scalar", sSP, d_avsp[1][pi_], av_l[:])
    avsp_done = sSP.v
    ckpts["att"] = p.checkpoint(ALLSEMS)

    barrier()
    wor = d_wos
    x1r = d_x1sp
    evac2 = []
    o_last = {}
    for duo in range(4):
        ldv = []
        p.wait("sync", sT, sT.v)
        for i in range(2):
            pi_ = 2 * duo + i
            for lvl in range(2):
                ldv.append(p.dma("sync", sOAV, oav_h[i][:] if lvl == 0 else
                                 oav_l[i][:], d_avsp[lvl][pi_]))
                ldv.append(p.dma("sync", sOAV, owo[i][lvl][:],
                                 wor[lvl][pi_ * 128:pi_ * 128 + 128, :]))
        for ti in range(NT):
            ps = PS[ti % 2]
            if duo == 0:
                if ("x1", ti % 2) in o_last:
                    p.wait("sync", sV, o_last[("x1", ti % 2)])
                x1ld = p.dma("sync", sX1[ti % 2], x1st[ti % 2][:, 0:DM], x1r[ti])
            if ti % 2 in o_last:
                p.wait("tensor", sV, o_last[ti % 2])
            p.wait("tensor", sOAV, ldv[-1])
            for half in range(2):
                k = 0
                last = 6 + (2 if duo == 0 else 0)
                for i in range(2):
                    for ax, wl in ((oav_h[i], 0), (oav_l[i], 0), (oav_h[i], 1)):
                        k += 1
                        p.op("tensor", sT, "matmul",
                             ps[0:128, half * 512:half * 512 + 512],
                             ax[:, ti * 128:ti * 128 + 128],
                             owo[i][wl][:, half * 512:half * 512 + 512],
                             start=(k == 1), stop=(k == last))
                if duo == 0:
                    for lvl in range(2):
                        k += 1
                        p.op("tensor", sT, "matmul",
                             ps[0:128, half * 512:half * 512 + 512],
                             c_onesb[32:33, 0:128],
                             obb[0:1, lvl * DM + half * 512:
                                 lvl * DM + half * 512 + 512],
                             start=False, stop=(k == last),
                             tile_position=(32, 0))
            tmm = sT.v
            p.wait("vector", sT, tmm)
            if duo == 0:
                p.wait("vector", sX1[ti % 2], x1ld)
                p.op("vector", sV, "tensor_tensor",
                     x2_res[:, ti * DM:(ti + 1) * DM], ps[0:128, 0:1024],
                     x1st[ti % 2][:, 0:DM], OP.add)
            else:
                p.op("vector", sV, "tensor_tensor",
                     x2_res[:, ti * DM:(ti + 1) * DM], ps[0:128, 0:1024],
                     x2_res[:, ti * DM:(ti + 1) * DM], OP.add)
            o_last[ti % 2] = sV.v
            o_last[("x1", ti % 2)] = sV.v

    if "x2d" in DBG.get("expose", ()):
        p.wait("scalar", sV, sV.v)
        p.dma("scalar", sSP, d_x2d[:].rearrange("t p d -> p t d"),
              x2_res[:].rearrange("p (t d) -> p t d", t=NT))

    ckpts["o"] = p.checkpoint(ALLSEMS)
    barrier()
    emit_ln_ffn(2)
    p.wait("sync", sSP, sSP.v)
    p.wait("sync", sV, sV.v)
    if DBG.get("stop") in ckpts:
        p.truncate(ckpts[DBG["stop"]])

    with nc.Block() as block:
        for eng in ENGS:
            def mk(e):
                def run(q):
                    for fn in p.cmds[e]:
                        fn(q)
                return run
            getattr(block, eng)(mk(eng))
    return nc, es


def _pack_inputs(inputs, c):
    rb = c % 2
    b = c // 2
    x = np.asarray(inputs["L0_attn_x"])
    tiles = x[b].reshape(32, 128, DM)
    x_own = np.ascontiguousarray(
        np.concatenate([tiles[0:1], tiles[rb::2]], axis=0))
    W1 = np.asarray(inputs["L0_ffn_W1"])
    W2 = np.asarray(inputs["L0_ffn_W2"])
    Qw = np.asarray(inputs["L0_attn_Q_w"])
    Kw = np.asarray(inputs["L0_attn_K_w"])
    Vw = np.asarray(inputs["L0_attn_V_w"])
    Ow = np.asarray(inputs["L0_attn_O_w"])
    m = {
        "x_own": x_own,
        "w1t": np.ascontiguousarray(W1.T),
        "w2t": np.ascontiguousarray(W2.T),
        "b1p": np.ascontiguousarray(
            np.asarray(inputs["L0_ffn_b1"]).reshape(32, 128).T),
        "b2r": np.asarray(inputs["L0_ffn_b2"]).reshape(1, DM),
        "wqp": np.ascontiguousarray(Qw.transpose(1, 0, 2).reshape(DM, DM)),
        "wkp": np.ascontiguousarray(Kw.transpose(1, 0, 2).reshape(DM, DM)),
        "wvp": np.ascontiguousarray(Vw.transpose(1, 0, 2).reshape(DM, DM)),
        "wop": np.ascontiguousarray(Ow.reshape(DM, DM)),
        "qbp": np.ascontiguousarray(
            np.asarray(inputs["L0_attn_Q_b"]).reshape(8, 128).T),
        "kbp": np.ascontiguousarray(
            np.asarray(inputs["L0_attn_K_b"]).reshape(8, 128).T),
        "vbr": np.asarray(inputs["L0_attn_V_b"]).reshape(1, DM),
        "obr": np.asarray(inputs["L0_attn_O_b"]).reshape(1, DM),
        "screp": np.ascontiguousarray(np.broadcast_to(
            np.asarray(inputs["L0_ffn_scale"]).reshape(1, 1), (128, 1))),
        "identb": np.eye(128, dtype=ml_dtypes.bfloat16),
        "qkmask": _mk_masks(rb),
    }
    return m


def _mk_masks(rb):
    kk = np.arange(128).reshape(128, 1)
    qq = np.arange(128).reshape(1, 128)
    msk = np.zeros((9, 128, 512), np.float32)
    for run in range(2):
        for i in range(4):
            slot = run * 4 + i
            for ip in range(4):
                kglob = 128 * (2 * i + run) + kk
                qglob = 128 * (2 * ip + rb) + qq
                msk[slot][:, ip * 128:(ip + 1) * 128] = np.where(
                    kglob > qglob, NEGM, 0.0)
    msk[8][:, 0:128] = np.where(kk > qq, NEGM, 0.0)
    return msk.astype(ml_dtypes.bfloat16)


NEGM = -100000.0
_CACHED = {}


def kernel(**inputs):
    if "nc" not in _CACHED:
        _CACHED["nc"], _CACHED["es"] = build_program()
    nc = _CACHED["nc"]
    in_maps = [_pack_inputs(inputs, c) for c in range(8)]
    res = run_bass_kernel_spmd(nc, in_maps, core_ids=list(range(8)))
    out = np.empty((B, S, DM), np.float32)
    for c in range(8):
        rb, b = c % 2, c // 2
        out[b].reshape(32, 128, DM)[rb::2] = np.asarray(
            res.results[c]["out_own"])
    return out



# version 03
# speedup vs baseline: 1.5827x; 1.0424x over previous
import numpy as np
import ml_dtypes
from contextlib import ExitStack
import concourse.bass as bass
import concourse.mybir as mybir
from concourse.bass_utils import run_bass_kernel_spmd

F32 = mybir.dt.float32
BF16 = mybir.dt.bfloat16
U8 = mybir.dt.uint8
AF = mybir.ActivationFunctionType
AX = mybir.AxisListType
OP = mybir.AluOpType

B, S, DM, HF, NH, DH = 4, 4096, 1024, 4096, 16, 64
NT = 17
DC = 8
HC = 32
NPAIR = 8
EXPC = 16.0
HALF_COLS = 1152

ENGS = ("sync", "tensor", "vector", "scalar", "gpsimd")
DBG = {}


class Sem:
    __slots__ = ("h", "v")

    def __init__(self, h):
        self.h = h
        self.v = 0


class P:
    def __init__(self, nc, es):
        self.nc = nc
        self.es = es
        self.cmds = {e: [] for e in ENGS}

    def sem(self, name):
        return Sem(self.es.enter_context(self.nc.semaphore(name)))

    def emit(self, eng, fn):
        self.cmds[eng].append(fn)

    def wait(self, eng, sem, val=None):
        v = sem.v if val is None else val
        if v <= 0:
            return
        self.emit(eng, lambda q, h=sem.h, v=v: q.wait_ge(h, v))

    def dma(self, eng, sem, out, in_, **kw):
        sem.v += 16

        def fn(q, out=out, in_=in_, kw=kw, h=sem.h):
            q.dma_start(out=out, in_=in_, **kw).then_inc(h, 16)
        self.emit(eng, fn)
        return sem.v

    def dmat(self, eng, sem, out, in_):
        sem.v += 16

        def fn(q, out=out, in_=in_, h=sem.h):
            q.dma_start_transpose(out, in_).then_inc(h, 16)
        self.emit(eng, fn)
        return sem.v

    def op(self, eng, sem, meth, *args, **kw):
        if eng in ("vector", "scalar", "gpsimd") and sem.v > 0:
            self.wait(eng, sem, sem.v)
        sem.v += 1

        def fn(q, meth=meth, args=args, kw=kw, h=sem.h):
            getattr(q, meth)(*args, **kw).then_inc(h, 1)
        self.emit(eng, fn)
        return sem.v

    def opq(self, eng, meth, *args, **kw):
        def fn(q, meth=meth, args=args, kw=kw):
            getattr(q, meth)(*args, **kw)
        self.emit(eng, fn)

    def checkpoint(self, sems):
        return ({e: len(self.cmds[e]) for e in ENGS},
                [(s, s.v) for s in sems])

    def truncate(self, snap):
        counts, semvals = snap
        for e in ENGS:
            self.cmds[e] = self.cmds[e][:counts[e]]
        for s, v in semvals:
            self.wait("sync", s, v)


def build_program():
    nc = bass.Bass("TRN2", target_bir_lowering=False)
    es = ExitStack()
    p = P(nc, es)

    d_x = nc.dram_tensor("x_own", [NT, 128, DM], F32, kind="ExternalInput").ap()
    d_w1t = nc.dram_tensor("w1t", [DM, HF], F32, kind="ExternalInput").ap()
    d_w2t = nc.dram_tensor("w2t", [HF, DM], F32, kind="ExternalInput").ap()
    d_b1 = nc.dram_tensor("b1p", [128, HC], F32, kind="ExternalInput").ap()
    d_b2 = nc.dram_tensor("b2r", [1, DM], F32, kind="ExternalInput").ap()
    d_wq = nc.dram_tensor("wqp", [DM, DM], F32, kind="ExternalInput").ap()
    d_wk = nc.dram_tensor("wkp", [DM, DM], F32, kind="ExternalInput").ap()
    d_wv = nc.dram_tensor("wvp", [DM, DM], F32, kind="ExternalInput").ap()
    d_wo = nc.dram_tensor("wop", [DM, DM], F32, kind="ExternalInput").ap()
    d_qb = nc.dram_tensor("qbp", [128, NPAIR], F32, kind="ExternalInput").ap()
    d_kb = nc.dram_tensor("kbp", [128, NPAIR], F32, kind="ExternalInput").ap()
    d_vb = nc.dram_tensor("vbr", [1, DM], F32, kind="ExternalInput").ap()
    d_ob = nc.dram_tensor("obr", [1, DM], F32, kind="ExternalInput").ap()
    d_sc = nc.dram_tensor("screp", [128, 1], F32, kind="ExternalInput").ap()
    d_id = nc.dram_tensor("identb", [128, 128], BF16, kind="ExternalInput").ap()
    d_msk = nc.dram_tensor("qkmask", [9, 128, 512], BF16, kind="ExternalInput").ap()
    d_out = nc.dram_tensor("out_own", [16, 128, DM], F32, kind="ExternalOutput").ap()

    def internal(name, shape, dt):
        if name in DBG.get("expose", ()):
            return nc.dram_tensor(name, shape, dt, kind="ExternalOutput").ap()
        return nc.dram_tensor(name, shape, dt).ap()

    d_w1s = internal("w1s", [2, DM, HF], BF16)
    d_w2s = internal("w2s", [2, HF, DM], BF16)
    d_wqs = internal("wqs", [2, DM, DM], BF16)
    d_wks = internal("wks", [2, DM, DM], BF16)
    d_wvs = internal("wvs", [2, DM, DM], BF16)
    d_wos = internal("wos", [2, DM, DM], BF16)
    d_agin = [internal("agin_a", [2, DC, 128, 1024], BF16),
              internal("agin_b", [2, DC, 128, 1024], BF16)]
    d_agout = [internal("agout_a", [2, 2, DC, 128, 1024], BF16),
               internal("agout_b", [2, 2, DC, 128, 1024], BF16)]
    d_xnown = internal("xnown", [2, DC, 128, NT * 128], BF16)
    d_x1sp = internal("x1sp", [NT, 128, DM], F32)
    d_vtm = internal("vtm", [2, 32, 128, DM], BF16)
    d_avsp = internal("avsp", [2, NPAIR, 128, NT * 128], BF16)
    d_x2d = internal("x2d", [NT, 128, DM], F32)

    sLD = p.sem("sLD")
    sSP = p.sem("sSP")
    sTR = p.sem("sTR")
    sV = p.sem("sV")
    sC = p.sem("sC")
    sT = p.sem("sT")
    sAG = p.sem("sAG")
    sG = p.sem("sG")
    sWS = [p.sem("sWS0"), p.sem("sWS1")]
    sWST = p.sem("sWST")
    sX = p.sem("sX")
    sW1 = [p.sem(f"sW1_{i}") for i in range(4)]
    sW2 = [p.sem(f"sW2_{i}") for i in range(8)]
    sXS = [p.sem("sXS0"), p.sem("sXS1")]
    sVBf = [p.sem("sVB0"), p.sem("sVB1")]
    sWQK = p.sem("sWQK")
    sWVB = p.sem("sWVB")
    sMSK = p.sem("sMSK")
    sOAV = p.sem("sOAV")
    sX1 = [p.sem("sX1_0"), p.sem("sX1_1")]
    sTRid = p.sem("sTRid")
    sTRlt = p.sem("sTRlt")
    sTRg = p.sem("sTRg")
    sTRe = p.sem("sTRe")
    sTRq = p.sem("sTRq")
    sTRt = p.sem("sTRt")
    sAGIN = [p.sem("sAGIN0"), p.sem("sAGIN1")]

    PS = [es.enter_context(nc.psum_tensor(f"P{i}", [128, 1024], F32))
          for i in range(4)]

    def sbuf(name, shape, dt):
        return es.enter_context(nc.sbuf_tensor(name, shape, dt))

    c_eps = sbuf("c_eps", [128, 1], F32)
    c_expb = sbuf("c_expb", [128, 1], F32)
    c_ones1 = sbuf("c_ones1", [128, 1], BF16)
    c_onesf = sbuf("c_onesf", [128, 64], F32)
    c_onesb = sbuf("c_onesb", [128, 128], BF16)
    identb = sbuf("identb_s", [128, 128], BF16)
    b1p = sbuf("b1p_s", [128, HC], F32)
    qbp = sbuf("qbp_s", [128, NPAIR], F32)
    kbp = sbuf("kbp_s", [128, NPAIR], F32)
    screp = sbuf("screp_s", [128, 1], F32)
    lnsm = sbuf("lnsm", [128, 96], F32)
    srow = sbuf("srow", [128, 512], F32)
    lt_fm = sbuf("lt_fm", [128, DC * 16], F32)
    lt_fm_h = sbuf("lt_fm_h", [128, DC * 16], BF16)
    lt_fm_l = sbuf("lt_fm_l", [128, DC * 16], BF16)

    SMB = 38016
    smb_t = es.enter_context(nc.sbuf_tensor("smalls", [128, SMB], U8))
    smb = smb_t.ap()

    def smv(p0, np_, nbytes, dt, off=0):
        return smb[p0:p0 + np_, off:off + nbytes].bitcast(dt)

    HW = 1152
    identf = smv(0, 16, 64, F32)
    t4f = smv(0, 4, 4 * HW * 4, F32, 64)
    t16f = smv(32, 16, HW * 4, F32)
    t16h = smv(32, 16, HW * 2, BF16, HW * 4)
    t16l = smv(32, 16, HW * 2, BF16, HW * 6)
    gstage = smv(0, 1, DM * 4, F32, 64)
    gq = smv(0, 1, DM * 4, F32, 64 + DM * 4)
    e2d = smv(32, 16, DM * 4, F32, HW * 8 + DM * 4)
    e2h = smv(32, 16, DM * 2, BF16, HW * 8 + DM * 8)
    e2l = smv(32, 16, DM * 2, BF16, HW * 8 + DM * 10)
    obf = smv(32, 1, DM * 4, F32, HW * 8 + DM * 12)
    obb = smv(32, 1, DM * 4, BF16, HW * 8 + DM * 16)
    b2f = smv(32, 1, DM * 4, F32, HW * 8 + DM * 18)
    b2b = smv(32, 1, DM * 4, BF16, HW * 8 + DM * 22)
    e1qf = smv(64, 4, DM * 4, F32)
    e1qh = smv(64, 4, DM * 2, BF16, DM * 4)
    e1ql = smv(64, 4, DM * 2, BF16, DM * 6)
    lt_tm = smv(64, 16, DM * 4, F32, DM * 8)
    p4f = smv(64, 4, HW * 4, F32, DM * 12)
    p4h = smv(64, 4, HW * 2, BF16, DM * 12 + HW * 4)
    p4l = smv(64, 4, HW * 2, BF16, DM * 12 + HW * 6)
    vbf = smv(64, 1, DM * 4, F32, DM * 12 + HW * 8)
    vbb = smv(64, 1, DM * 4, BF16, DM * 16 + HW * 8)
    identf = smv(64, 16, 64, F32, DM * 16 + HW * 8 + DM * 4)
    identf0 = smv(0, 16, 64, F32)

    POOLB = 161 * 1024
    pool_t = es.enter_context(nc.sbuf_tensor("pool", [128, POOLB], U8))
    pl = pool_t.ap()

    def bf(a, b):
        return pl[:, a:b].bitcast(BF16)

    def f32(a, b):
        return pl[:, a:b].bitcast(F32)

    x_res = f32(0, 69632)
    R2 = 69632
    xn_h = bf(R2, R2 + HW * DC * 2)
    xn_l = bf(R2 + HW * DC * 2, R2 + HW * DC * 4)
    F3 = R2 + HW * DC * 4
    w1b = [bf(F3 + i * 2048, F3 + (i + 1) * 2048) for i in range(4)]
    w2b = [bf(F3 + 8192 + i * 2048, F3 + 8192 + (i + 1) * 2048) for i in range(8)]
    F4 = F3 + 24576
    h_h = bf(F4, F4 + 4 * HW * 2)
    h_l = bf(F4 + 4 * HW * 2, F4 + 8 * HW * 2)
    F5 = F4 + 8 * HW * 2
    gel = f32(F5, F5 + 2048)
    ln_stage = f32(F5 + 2048, F5 + 6144)
    ln_hi = bf(F5 + 6144, F5 + 8192)
    ln_lo = bf(F5 + 8192, F5 + 10240)
    sq_scr = f32(F5 + 10240, F5 + 14336)
    FFN_END = F5 + 14336
    assert FFN_END <= POOLB, FFN_END

    A0 = 0
    kf_h = bf(A0, A0 + 8192)
    kf_l = bf(A0 + 8192, A0 + 16384)
    vb_h = bf(A0 + 16384, A0 + 24576)
    vb_l = bf(A0 + 24576, A0 + 32768)
    q_h = bf(A0 + 32768, A0 + 37120)
    q_l = bf(A0 + 37120, A0 + 41472)
    ef32 = [f32(A0 + 41472, A0 + 45568), f32(A0 + 45568, A0 + 49664), None]
    ehl = [bf(A0 + 49664, A0 + 53760), bf(A0 + 53760, A0 + 57856), None]
    A1 = A0 + 57856
    xs_b = [[bf(A1 + (2 * i + l) * 8192, A1 + (2 * i + l + 1) * 8192)
             for l in range(2)] for i in range(2)]
    A2 = A1 + 32768
    wqk = [[bf(A2 + (2 * i + l) * 2048, A2 + (2 * i + l + 1) * 2048)
            for l in range(2)] for i in range(2)]
    A3 = A2 + 8192
    wvb = [bf(A3 + l * 16384, A3 + (l + 1) * 16384) for l in range(2)]
    A4 = A3 + 32768
    masks = bf(A4, A4 + 9216)
    A5 = A4 + 9216
    av_f = f32(A5, A5 + 2048)
    avn_f = f32(A5 + 2048, A5 + 4096)
    av_h = bf(A5 + 4096, A5 + 8448)
    av_l = bf(A5 + 8448, A5 + 12800)
    A6 = A5 + 12800
    vstage = f32(A6, A6 + 4096)
    vsth = bf(A6 + 4096, A6 + 6144)
    vstl = bf(A6 + 6144, A6 + 8192)
    kqstage = f32(A6 + 8192, A6 + 10240)
    ef32[2] = f32(A6, A6 + 4096)
    ehl[2] = bf(A6 + 4096, A6 + 8192)
    ATT_END = A6 + 10240
    assert ATT_END <= POOLB

    x2_res = f32(0, 69632)
    O1 = 69632
    oav_h = [bf(O1 + i * 4352, O1 + (i + 1) * 4352) for i in range(2)]
    oav_l = [bf(O1 + 8704 + i * 4352, O1 + 8704 + (i + 1) * 4352) for i in range(2)]
    O2 = O1 + 17408
    owo = [[bf(O2 + (2 * i + l) * 2048, O2 + (2 * i + l + 1) * 2048)
            for l in range(2)] for i in range(2)]
    O3 = O2 + 8192
    x1st = [f32(O3 + i * 4096, O3 + (i + 1) * 4096) for i in range(2)]
    O_END = O3 + 8192
    assert O_END <= POOLB

    def barrier():
        vals = [(s, s.v) for s in ([sV, sC, sT, sG, sLD, sSP, sTR, sWST, sX,
                 sWQK, sWVB, sMSK, sOAV, sTRid, sTRlt, sTRg, sTRe, sTRq, sTRt]
                + sWS + sW1 + sW2 + sXS + sVBf + sX1 + sAGIN)]
        for eng in ENGS:
            for s, v in vals:
                p.wait(eng, s, v)

    def split(src, hi, lo):
        p.op("vector", sV, "tensor_copy", hi, src)
        p.op("vector", sV, "tensor_tensor", lo, src, hi, OP.subtract)
        return sV.v

    ALLSEMS = ([sLD, sSP, sTR, sV, sC, sT, sG, sWST, sX, sWQK, sWVB, sMSK,
                sOAV, sTRid, sTRlt, sTRg, sTRe, sTRq, sTRt]
               + sWS + sW1 + sW2 + sXS + sVBf + sX1 + sAGIN)
    ckpts = {}
    p.op("vector", sV, "memset", c_eps[:], 1e-5)
    p.op("vector", sV, "memset", c_expb[:], -EXPC)
    p.op("vector", sV, "memset", c_ones1[:], 1.0)
    p.op("vector", sV, "memset", c_onesf[:], 1.0)
    p.op("vector", sV, "memset", c_onesb[:], 1.0)

    p.dma("sync", sLD, identb[:], d_id[:])
    p.dma("sync", sLD, b1p[:], d_b1[:])
    p.dma("sync", sLD, qbp[:], d_qb[:])
    p.dma("sync", sLD, kbp[:], d_kb[:])
    p.dma("sync", sLD, screp[:], d_sc[:])
    p.dma("sync", sLD, b2f[0:1, 0:DM], d_b2[:])
    p.dma("sync", sLD, obf[0:1, 0:DM], d_ob[:])
    t = p.dma("sync", sLD, vbf[0:1, 0:DM], d_vb[:])
    CONSTS_LD = sLD.v
    p.wait("vector", sLD, CONSTS_LD)
    p.op("vector", sV, "tensor_copy", identf0[:], identb[0:16, 0:16])
    p.wait("gpsimd", sV, sV.v)
    p.dma("gpsimd", sTRid, identf[:], identf0[:])
    for rf, rbp in ((b2f, b2b), (obf, obb), (vbf, vbb)):
        split(rf[0:1, 0:DM], rbp[0:1, 0:DM], rbp[0:1, DM:2 * DM])

    wsf = [f32(R2, R2 + 16384), f32(R2 + 16384, R2 + 32768)]
    wsh = bf(R2 + 32768, R2 + 40960)
    wsl = bf(R2 + 40960, R2 + 49152)
    jobs = []
    w1v = d_w1t.rearrange("(dc p) h -> dc p h", p=128)
    w1s0 = d_w1s[0].rearrange("(dc p) h -> dc p h", p=128)
    w1s1 = d_w1s[1].rearrange("(dc p) h -> dc p h", p=128)
    for i in range(DC):
        jobs.append((w1v[i], w1s0[i], w1s1[i], HF))
    w2v = d_w2t.rearrange("(hc p) d -> hc p d", p=128)
    w2s0 = d_w2s[0].rearrange("(hc p) d -> hc p d", p=128)
    w2s1 = d_w2s[1].rearrange("(hc p) d -> hc p d", p=128)
    for i in range(HC):
        jobs.append((w2v[i], w2s0[i], w2s1[i], DM))
    for dsrc, ddst in ((d_wq, d_wqs), (d_wk, d_wks), (d_wv, d_wvs), (d_wo, d_wos)):
        sv_ = dsrc.rearrange("(dc p) c -> dc p c", p=128)
        dv0 = ddst[0].rearrange("(dc p) c -> dc p c", p=128)
        dv1 = ddst[1].rearrange("(dc p) c -> dc p c", p=128)
        for i in range(DC):
            jobs.append((sv_[i], dv0[i], dv1[i], DM))
    wv_done, wst_done = [], []
    CKJOBS = DBG.get("njobs", len(jobs))
    jobs = jobs[:CKJOBS]
    for i, (src, dsth, dstl, width) in enumerate(jobs):
        buf = wsf[i % 2]
        if i >= 2:
            p.wait("sync", sV, wv_done[i - 2])
        t = p.dma("sync", sWS[i % 2], buf[:, 0:width], src)
        p.wait("vector", sWS[i % 2], t)
        if i >= 1:
            p.wait("vector", sWST, wst_done[i - 1])
        p.op("vector", sV, "tensor_copy", wsh[:, 0:width], buf[:, 0:width])
        p.op("vector", sV, "tensor_tensor", wsl[:, 0:width], buf[:, 0:width],
             wsh[:, 0:width], OP.subtract)
        wv_done.append(sV.v)
        p.wait("scalar", sV, sV.v)
        p.dma("scalar", sWST, dsth, wsh[:, 0:width])
        wst_done.append(p.dma("scalar", sWST, dstl, wsl[:, 0:width]))

    def emit_ln_batch(tiles, ncols, lora_after_slot0=False, xn_guard=False):
        nt_ = len(tiles)
        if xn_guard:
            p.wait("scalar", sT, sT.v)
        p.wait("scalar", sV, sV.v)
        ssum = lnsm[:, 0:nt_]
        sqsum = lnsm[:, 32:32 + nt_]
        mm = lnsm[:, 64:64 + nt_]
        rr = lnsm[:, 80:80 + nt_]
        for i, (xo, _) in enumerate(tiles):
            p.op("vector", sV, "reduce_sum", lnsm[:, i:i + 1],
                 x_res[:, xo:xo + DM], axis=AX.X)
            p.op("scalar", sC, "activation", sq_scr[:, 0:DM],
                 x_res[:, xo:xo + DM], AF.Square,
                 accum_out=lnsm[:, 32 + i:32 + i + 1])
        p.wait("vector", sV, sV.v)
        p.wait("vector", sC, sC.v)
        p.op("vector", sV, "tensor_scalar_mul", mm, ssum, 1.0 / DM)
        p.op("vector", sV, "tensor_mul", rr, mm, mm)
        p.op("vector", sV, "tensor_scalar", lnsm[:, 48:48 + nt_], sqsum,
             1.0 / DM, None, OP.mult)
        p.op("vector", sV, "tensor_tensor", rr, lnsm[:, 48:48 + nt_], rr,
             OP.subtract)
        p.wait("scalar", sV, sV.v)
        p.op("scalar", sC, "activation", rr, rr, AF.Sqrt, bias=c_eps[:, 0:1],
             scale=1.0)
        p.wait("vector", sC, sC.v)
        p.op("vector", sV, "reciprocal", rr, rr)
        p.wait("vector", sV, sV.v)
        for i, (xo, lsl) in enumerate(tiles):
            p.wait("vector", sSP, sSP.v)
            p.op("vector", sV, "tensor_scalar", ln_stage[:, 0:DM],
                 x_res[:, xo:xo + DM], mm[:, i:i + 1], rr[:, i:i + 1],
                 OP.subtract, OP.mult)
            split(ln_stage[:, 0:DM], ln_hi[:, 0:DM], ln_lo[:, 0:DM])
            vs = sV.v
            if lsl is not None:
                p.wait("scalar", sV, vs)
                for dc in range(DC):
                    p.dmat("scalar", sSP,
                           xn_h[:, dc * ncols + lsl * 128:
                                dc * ncols + lsl * 128 + 128],
                           ln_hi[:, dc * 128:dc * 128 + 128])
                    p.dmat("scalar", sSP,
                           xn_l[:, dc * ncols + lsl * 128:
                                dc * ncols + lsl * 128 + 128],
                           ln_lo[:, dc * 128:dc * 128 + 128])
            if lora_after_slot0 and i == 0:
                emit_lora_prep()
        return sV.v

    def emit_lora_prep():
        p.wait("gpsimd", sV, sV.v)
        t = p.dma("gpsimd", sTRlt, lt_tm[0:16, 0:DM], ln_stage[1:17, 0:DM])
        p.wait("tensor", sTRlt, t)
        p.wait("tensor", sTRid, 16)
        for dc in range(DC):
            p.op("tensor", sT, "transpose", PS[3][0:128, 512:528],
                 lt_tm[0:16, dc * 128:dc * 128 + 128], identf[:])
            p.wait("vector", sT, sT.v)
            p.op("vector", sV, "tensor_copy", lt_fm[:, dc * 16:dc * 16 + 16],
                 PS[3][0:128, 512:528])
            p.wait("tensor", sV, sV.v)
        split(lt_fm[:], lt_fm_h[:], lt_fm_l[:])
        p.op("vector", sV, "memset", e2d[:], 0.0)
        for i in range(4):
            p.wait("gpsimd", sV, sV.v)
            t_ = p.dma("gpsimd", sTRg, gstage[0:1, 0:DM],
                       lt_tm[8 + 2 * i:8 + 2 * i + 1, 0:DM])
            p.wait("vector", sTRg, t_)
            p.wait("vector", sTRe, sTRe.v)
            p.op("vector", sV, "tensor_copy",
                 gq[0:1, 0:DM].rearrange("p (q r) -> p q r", q=4),
                 gstage[0:1, 0:DM].rearrange("p (r q) -> p q r", q=4))
            p.wait("gpsimd", sV, sV.v)
            p.dma("gpsimd", sTRe,
                  e2d[4 * i:4 * i + 4, i * 256:(i + 1) * 256],
                  gq[0:1, 0:DM].rearrange("p (q r) -> p q r", q=4))
        p.wait("vector", sTRe, sTRe.v)
        split(e2d[:], e2h[:], e2l[:])
        return sV.v

    def emit_ffn_half(slots, out_phase):
        ncols = len(slots) * 128
        if ncols == 1152:
            grps = [(0, 128), (128, 512), (640, 512)]
        else:
            assert ncols == 1024
            grps = [(0, 512), (512, 512)]
        lora_v = sV.v
        for (o, w) in grps:
            p.wait("tensor", sV, lora_v)
            n3 = DC * 3
            k = 0
            for dc in range(DC):
                for pi, (lh, rh) in enumerate(((lt_fm_h, xn_h), (lt_fm_h, xn_l),
                                               (lt_fm_l, xn_h))):
                    k += 1
                    p.op("tensor", sT, "matmul", PS[3][64:68, 512:512 + w],
                         lh[:, dc * 16 + 1:dc * 16 + 9:2],
                         rh[:, dc * ncols + o:dc * ncols + o + w],
                         start=(k == 1), stop=(k == n3),
                         tile_position=(0, 64))
            p.wait("vector", sT, sT.v)
            p.op("vector", sV, "tensor_scalar", p4f[0:4, o:o + w],
                 PS[3][64:68, 512:512 + w], screp[64:68, 0:1], None, OP.mult)
            p.wait("tensor", sV, sV.v)
        split(p4f[0:4, 0:ncols], p4h[0:4, 0:ncols], p4l[0:4, 0:ncols])
        p.op("vector", sV, "memset", t4f[0:4, 0:4 * ncols], 0.0)
        vprep = sV.v
        p.wait("tensor", sV, vprep)

        m1ps = [(PS[0], 0), (PS[0], 512)]
        m1_gelu_done = {}
        m1ctr = 0
        w1_ld, w2_ld = {}, {}
        w1use, w2use = {}, {}
        w1r = d_w1s.rearrange("l (dc p) h -> l p dc h", p=128)
        w2r = d_w2s.rearrange("l (hc p) d -> l hc p d", p=128)

        def load_w1(hc):
            for lvl in range(2):
                slot = (2 * hc + lvl) % 4
                if slot in w1use:
                    p.wait("sync", sT, w1use[slot])
                p.dma("sync", sW1[slot],
                      w1b[slot][:].rearrange("p (dc c) -> p dc c", dc=DC),
                      w1r[lvl][:, :, hc * 128:hc * 128 + 128])
                w1_ld[(hc, lvl)] = (slot, sW1[slot].v)

        def load_w2(hc):
            for lvl in range(2):
                slot = (2 * hc + lvl) % 8
                if slot in w2use:
                    p.wait("sync", sT, w2use[slot])
                p.dma("sync", sW2[slot], w2b[slot][:], w2r[lvl][hc])
                w2_ld[(hc, lvl)] = (slot, sW2[slot].v)

        load_w1(0)
        load_w1(1)
        for hc0 in range(4):
            load_w2(hc0)
        m2ps = [PS[2], PS[3]]
        m2_last = {}
        t_evac_last = {}
        tctr = [0]

        e1q_v = 0
        NBLK = DBG.get('nblk', 8)
        for blk in range(NBLK):
            if blk % 2 == 0:
                qt = blk // 2
                p.wait("vector", sT, sT.v)
                p.op("vector", sV, "memset", e1qf[:], 0.0)
                p.wait("gpsimd", sV, sV.v)
                t_ = p.dma("gpsimd", sTRq, e1qf[qt:qt + 1, 0:DM],
                           lt_tm[2 * qt:2 * qt + 1, 0:DM])
                p.wait("vector", sTRq, t_)
                split(e1qf[0:4, 0:DM], e1qh[0:4, 0:DM], e1ql[0:4, 0:DM])
                e1q_v = sV.v
            for ci in range(4):
                hc = blk * 4 + ci
                s1h, t1h = w1_ld[(hc, 0)]
                s1l, t1l = w1_ld[(hc, 1)]
                p.wait("tensor", sW1[s1h], t1h)
                p.wait("tensor", sW1[s1l], t1l)
                for (o, w) in grps:
                    ps, po = m1ps[m1ctr % 2]
                    if m1ctr - 2 in m1_gelu_done:
                        p.wait("tensor", sC, m1_gelu_done[m1ctr - 2])
                    nmm = 3 * DC + (0 if DBG.get("no_m1lora") else 3)
                    k = 0
                    for pi, (wsl_, rh) in enumerate(((s1h, xn_h), (s1h, xn_l),
                                                     (s1l, xn_h))):
                        for dc in range(DC):
                            k += 1
                            p.op("tensor", sT, "matmul", ps[0:128, po:po + w],
                                 w1b[wsl_][:, dc * 128:dc * 128 + 128],
                                 rh[:, dc * ncols + o:dc * ncols + o + w],
                                 start=(k == 1), stop=(k == nmm))
                    p.wait("tensor", sV, e1q_v)
                    if DBG.get("no_m1lora"):
                        nmm = 3 * DC
                    for pi, (lh, rh) in enumerate(((e1qh, p4h), (e1qh, p4l),
                                                   (e1ql, p4h))):
                        if DBG.get("no_m1lora"):
                            break
                        k += 1
                        p.op("tensor", sT, "matmul", ps[0:128, po:po + w],
                             lh[0:4, (hc % 8) * 128:(hc % 8) * 128 + 128],
                             rh[0:4, o:o + w], start=False, stop=(k == nmm),
                             tile_position=(64, 0))
                    tmm = sT.v
                    w1use[s1h] = tmm
                    w1use[s1l] = tmm
                    p.wait("scalar", sT, tmm)
                    p.wait("scalar", sV, sV.v)
                    p.op("scalar", sC, "activation", gel[:, 0:w],
                         ps[0:128, po:po + w], AF.Gelu,
                         bias=b1p[:, hc:hc + 1], scale=1.0)
                    m1_gelu_done[m1ctr] = sC.v
                    m1ctr += 1
                    p.wait("vector", sC, sC.v)
                    split(gel[:, 0:w],
                          h_h[:, ci * ncols + o:ci * ncols + o + w],
                          h_l[:, ci * ncols + o:ci * ncols + o + w])
                hs = sV.v
                quarter = hc // 8
                dcl = hc % 8
                p.wait("tensor", sV, hs)
                for (o, w) in grps:
                    tslot = tctr[0] % 2
                    tctr[0] += 1
                    if tslot in t_evac_last:
                        p.wait("tensor", sV, t_evac_last[tslot])
                    for pi, (lh, rh) in enumerate(((lt_fm_h, h_h), (lt_fm_h, h_l),
                                                   (lt_fm_l, h_h))):
                        p.op("tensor", sT, "matmul",
                             PS[1][0:4, tslot * 512:tslot * 512 + w],
                             lh[:, dcl * 16 + 9:dcl * 16 + 16:2],
                             rh[:, ci * ncols + o:ci * ncols + o + w],
                             start=(pi == 0), stop=(pi == 2))
                    p.wait("vector", sT, sT.v)
                    p.op("vector", sV, "scalar_tensor_tensor",
                         t4f[0:4, quarter * ncols + o:quarter * ncols + o + w],
                         PS[1][0:4, tslot * 512:tslot * 512 + w],
                         screp[0:4, 0:1],
                         t4f[0:4, quarter * ncols + o:quarter * ncols + o + w],
                         OP.mult, OP.add)
                    t_evac_last[tslot] = sV.v
                if hc + 2 < HC:
                    load_w1(hc + 2)
            for ti, sl in enumerate(slots):
                ps = m2ps[ti % 2]
                if ti % 2 in m2_last:
                    p.wait("tensor", sV, m2_last[ti % 2])
                nmm = 4 * 3 * 2
                k = 0
                for ci in range(4):
                    hc = blk * 4 + ci
                    s2h, t2h = w2_ld[(hc, 0)]
                    s2l, t2l = w2_ld[(hc, 1)]
                    if ti == 0:
                        p.wait("tensor", sW2[s2h], t2h)
                        p.wait("tensor", sW2[s2l], t2l)
                    for pi, (hh, wsl_) in enumerate(((h_h, s2h), (h_l, s2h),
                                                     (h_h, s2l))):
                        for half in range(2):
                            k += 1
                            p.op("tensor", sT, "matmul",
                                 ps[0:128, half * 512:half * 512 + 512],
                                 hh[:, ci * ncols + ti * 128:
                                    ci * ncols + ti * 128 + 128],
                                 w2b[wsl_][:, half * 512:half * 512 + 512],
                                 start=(ci == 0 and pi == 0),
                                 stop=(k in (nmm - 1, nmm)))
                    w2use[s2h] = sT.v
                    w2use[s2l] = sT.v
                tmm = sT.v
                p.wait("vector", sT, tmm)
                p.op("vector", sV, "scalar_tensor_tensor",
                     x_res[:, sl * DM:(sl + 1) * DM], ps[0:128, 0:1024], 0.5,
                     x_res[:, sl * DM:(sl + 1) * DM], OP.mult, OP.add)
                m2_last[ti % 2] = sV.v
            if blk + 1 < NBLK:
                for hc2 in range(4 * blk + 4, 4 * blk + 8):
                    load_w2(hc2)

        if DBG.get("no_m2x"):
            return
        NO_T16 = DBG.get("no_t16")
        NO_B2K = DBG.get("no_b2k")
        if not NO_T16:
            p.wait("gpsimd", sV, sV.v)
            for i in range(4):
                for q_ in range(4):
                    p.dma("gpsimd", sTRt,
                          t16f[4 * i + q_:4 * i + q_ + 1, 0:ncols],
                          t4f[i:i + 1, q_ * ncols:(q_ + 1) * ncols])
            p.wait("vector", sTRt, sTRt.v)
            split(t16f[0:16, 0:ncols], t16h[0:16, 0:ncols], t16l[0:16, 0:ncols])
        p.wait("tensor", sV, sV.v)
        for ti, sl in enumerate(slots):
            ps = m2ps[ti % 2]
            if ti % 2 in m2_last:
                p.wait("tensor", sV, m2_last[ti % 2])
            for half in range(2):
                k = 0
                nmm2 = (0 if NO_T16 else 3) + (0 if NO_B2K else 2)
                for pi, (th, eh) in enumerate(((t16h, e2h), (t16h, e2l),
                                               (t16l, e2h))):
                    if NO_T16:
                        break
                    k += 1
                    p.op("tensor", sT, "matmul",
                         ps[0:128, half * 512:half * 512 + 512],
                         th[0:16, ti * 128:ti * 128 + 128],
                         eh[0:16, half * 512:half * 512 + 512],
                         start=(k == 1), stop=(k == nmm2),
                         tile_position=(32, 0))
                for lvl in range(2):
                    if NO_B2K:
                        break
                    k += 1
                    p.op("tensor", sT, "matmul",
                         ps[0:128, half * 512:half * 512 + 512],
                         c_onesb[32:33, 0:128],
                         b2b[0:1, lvl * DM + half * 512:
                             lvl * DM + half * 512 + 512],
                         start=(k == 1), stop=(k == nmm2),
                         tile_position=(32, 0))
            tmm = sT.v
            p.wait("vector", sT, tmm)
            p.op("vector", sV, "scalar_tensor_tensor",
                 x_res[:, sl * DM:(sl + 1) * DM], ps[0:128, 0:1024], 0.5,
                 x_res[:, sl * DM:(sl + 1) * DM], OP.mult, OP.add)
            m2_last[ti % 2] = sV.v
            if out_phase:
                p.wait("scalar", sV, sV.v)
                p.dma("scalar", sSP, d_out[sl - 1],
                      x_res[:, sl * DM:(sl + 1) * DM])

    HALVES1 = [list(range(0, 9)), list(range(9, 17))]

    RGROUPS = [[0, 1]] if DBG.get("rg2") else [[0, 1], [2, 3], [4, 5], [6, 7]]

    def emit_ag_half(hi_, slots):
        ncols = len(slots) * 128
        emit_ln_batch([(sl * DM, li) for li, sl in enumerate(slots)], ncols,
                      xn_guard=True)
        base = slots[0]
        p.wait("scalar", sSP, sSP.v)
        for dc in range(DC):
            for lvl, src_ in ((0, xn_h), (1, xn_l)):
                p.dma("scalar", sSP,
                      d_xnown[lvl][dc][:, base * 128:base * 128 + ncols],
                      src_[:, dc * ncols:dc * ncols + ncols])
                lo = 1 if base == 0 else base
                off = (lo - base) * 128
                p.dma("scalar", sAGIN[hi_],
                      d_agin[hi_][lvl][dc][:, 0:ncols - off],
                      src_[:, dc * ncols + off:dc * ncols + ncols])
        p.wait("gpsimd", sAGIN[hi_], sAGIN[hi_].v)
        sAG.v += 1

        def fn_ag(q, h=sAG.h, t_in=d_agin[hi_], t_out=d_agout[hi_]):
            q.collective_compute(
                "AllGather", OP.bypass,
                replica_groups=RGROUPS,
                ins=[t_in.opt()], outs=[t_out.opt()],
            ).then_inc(h, 1)
        p.emit("gpsimd", fn_ag)
        p.wait("scalar", sV, sV.v)
        p.dma("scalar", sSP,
              d_x1sp[slots[0]:slots[-1] + 1].rearrange("t p d -> p t d"),
              x_res[:, slots[0] * DM:(slots[-1] + 1) * DM].rearrange(
                  "p (t d) -> p t d", t=len(slots)))

    def emit_ln_ffn(which):
        out_phase = (which == 2)
        for hi_, slots in enumerate(HALVES1):
            if ckpts is not None:
                ckpts[f"pre{which}{hi_}"] = p.checkpoint(ALLSEMS)
            ffn_slots = slots if which == 1 else [s for s in slots if s != 0]
            ncols = len(ffn_slots) * 128
            barrier()
            tiles = []
            for sl in slots:
                lsl = ffn_slots.index(sl) if sl in ffn_slots else None
                tiles.append((sl * DM, lsl))
            emit_ln_batch(tiles, ncols, lora_after_slot0=(0 in slots))
            if ckpts is not None:
                ckpts[f"ln{which}{hi_}"] = p.checkpoint(ALLSEMS)
            p.wait("tensor", sSP, sSP.v)
            emit_ffn_half(ffn_slots, out_phase)
            if which == 1:
                emit_ag_half(hi_, slots)
            if ckpts is not None:
                ckpts[f"ffnh{which}{hi_}"] = p.checkpoint(ALLSEMS)

    xld = []
    for tt in range(NT):
        xld.append(p.dma("sync", sX, x_res[:, tt * DM:(tt + 1) * DM], d_x[tt]))
    p.wait("vector", sX, sX.v)
    p.wait("vector", sWST, sWST.v)
    ckpts["wsplit"] = p.checkpoint(ALLSEMS)
    emit_ln_ffn(1)
    xnown_done = sSP.v
    ckpts["ffn1"] = p.checkpoint(ALLSEMS)
    ckpts["ag"] = p.checkpoint(ALLSEMS)

    p.wait("sync", sT, sT.v)
    p.wait("sync", sV, sV.v)
    p.wait("sync", sC, sC.v)
    p.wait("sync", sSP, sSP.v)
    for s_ in sAGIN:
        p.wait("sync", s_, s_.v)
    t = p.dma("sync", sMSK, masks[:].rearrange("p (m c) -> p m c", m=9),
              d_msk[:].rearrange("m p c -> p m c"))
    p.wait("tensor", sMSK, 16)
    wqr = d_wqs.rearrange("l (dc p) c -> l p dc c", p=128)
    wkr = d_wks.rearrange("l (dc p) c -> l p dc c", p=128)
    wvr = d_wvs.rearrange("l (dc p) c -> l p dc c", p=128)
    for lvl in range(2):
        p.dma("sync", sWVB, wvb[lvl][:].rearrange("p (dc c) -> p dc c", dc=DC),
              wvr[lvl])
    p.wait("tensor", sWVB, 32)
    xs_ld = {}

    def load_xs(tb, bufi):
        rank = tb // 4
        ab = (tb // 2) % 2
        off = (tb % 2) * 512
        p.wait("sync", sAG, 1 + ab)
        for lvl in range(2):
            p.dma(
                "sync", sXS[bufi],
                xs_b[bufi][lvl][:].rearrange("p (dc c) -> p dc c", dc=DC),
                d_agout[ab][rank][lvl][:, :, off:off + 512].rearrange(
                    "dc p c -> p dc c"))
            xs_ld[(tb, lvl)] = (bufi, sXS[bufi].v)

    VORDER = [0, 1, 4, 5, 2, 3, 6, 7]
    load_xs(VORDER[0], 0)
    load_xs(VORDER[1], 1)
    vps = [PS[0], PS[1]]
    vevac = []
    for tbi in range(8):
        tb = VORDER[tbi]
        bufi = tbi % 2
        p.wait("tensor", sXS[bufi], xs_ld[(tb, 1)][1])
        for g4 in range(4):
            g = tb * 4 + g4
            ps = vps[g % 2]
            if len(vevac) >= 2:
                p.wait("tensor", sV, vevac[-2])
            for half in range(2):
                k = 0
                for pi, (xl, wl) in enumerate(((0, 0), (1, 0), (0, 1))):
                    for dc in range(DC):
                        k += 1
                        p.op("tensor", sT, "matmul",
                             ps[0:128, half * 512:half * 512 + 512],
                             xs_b[bufi][xl][:, dc * 512 + g4 * 128:
                                            dc * 512 + g4 * 128 + 128],
                             wvb[wl][:, dc * 1024 + half * 512:
                                     dc * 1024 + half * 512 + 512],
                             start=(k == 1), stop=False)
                for lvl in range(2):
                    p.op("tensor", sT, "matmul",
                         ps[0:128, half * 512:half * 512 + 512],
                         c_onesb[64:65, 0:128],
                         vbb[0:1, lvl * DM + half * 512:
                             lvl * DM + half * 512 + 512],
                         start=False, stop=(lvl == 1), tile_position=(64, 0))
            tmm = sT.v
            p.wait("vector", sT, tmm)
            p.wait("vector", sSP, sSP.v)
            p.op("vector", sV, "tensor_copy", vstage[:, 0:1024], ps[0:128, 0:1024])
            split(vstage[:, 0:1024], vsth[:, 0:1024], vstl[:, 0:1024])
            vevac.append(sV.v)
            p.wait("scalar", sV, sV.v)
            p.dma("scalar", sSP, d_vtm[0][g], vsth[:, 0:1024])
            p.dma("scalar", sSP, d_vtm[1][g], vstl[:, 0:1024])
        if tbi + 2 < 8:
            p.wait("sync", sT, sT.v)
            load_xs(VORDER[tbi + 2], bufi)
    vspill_done = sSP.v
    ckpts["vproj"] = p.checkpoint(ALLSEMS)

    QGRPS = [(0, 128), (128, 512), (640, 512), (1152, 512), (1664, 512)]

    def klist_for(j):
        ks = []
        for g in range(4 * j + 4):
            ks.append((g, (g - 4 * j) if g >= 4 * j else None))
        for g in range(4 * j + 4):
            ks.append((16 + g, (4 + g - 4 * j) if g >= 4 * j else None))
        return ks

    vfetch_ranges = {
        "sh": [(0, 1)],
        0: [(1, 4), (16, 20)],
        1: [(4, 8), (20, 24)],
        2: [(8, 12), (24, 28)],
        3: [(12, 16), (28, 32)],
    }
    av_last_read = 0
    kq_last_use = 0
    vb_last_use = 0
    p.wait("sync", sSP, vspill_done)
    p.wait("sync", sSP, xnown_done)
    p.wait("gpsimd", sSP, vspill_done)
    p.wait("vector", sSP, vspill_done)
    for pi_ in range(NPAIR):
        wq_ld = []
        for mi, wsrc in ((0, wqr), (1, wkr)):
            for lvl in range(2):
                wq_ld.append(p.dma(
                    "sync", sWQK,
                    wqk[mi][lvl][:].rearrange("p (dc c) -> p dc c", dc=DC),
                    wsrc[lvl][:, :, pi_ * 128:pi_ * 128 + 128]))
        kq_last = {}
        mm_done = []

        def load_q(gi, bufi):
            o, w = QGRPS[gi]
            for lvl in range(2):
                p.dma(
                    "sync", sXS[bufi],
                    xs_b[bufi][lvl][:, 0:DC * 512].rearrange(
                        "p (dc c) -> p dc c", dc=DC)[:, :, 0:w],
                    d_xnown[lvl][:, :, o:o + w].rearrange("dc p c -> p dc c"))

        seq = [("k", tb) for tb in range(8)] + [("q", gi) for gi in range(5)]
        p.wait("sync", sT, sT.v)
        if seq[0][0] == "k":
            load_xs(0, 0)
        for si, (kind, idx) in enumerate(seq):
            bufi = si % 2
            if si + 1 < len(seq):
                if si >= 1:
                    p.wait("sync", sT, mm_done[si - 1])
                nk_, ni_ = seq[si + 1]
                if nk_ == "k":
                    load_xs(ni_, (si + 1) % 2)
                else:
                    load_q(ni_, (si + 1) % 2)
            p.wait("tensor", sXS[bufi], sXS[bufi].v if kind == "q" or idx > 0
                   else xs_ld[(0, 1)][1])
            p.wait("tensor", sWQK, wq_ld[-1])
            if kind == "k":
                o, w, mi = idx * 512, 512, 1
                ps = PS[2][0:128, (idx % 2) * 512:(idx % 2) * 512 + 512]
            else:
                o, w = QGRPS[idx]
                mi = 0
                ps = PS[3][0:128, (idx % 2) * 512:(idx % 2) * 512 + w]
            if (kind, idx % 2) in kq_last:
                p.wait("tensor", sV, kq_last[(kind, idx % 2)])
            k = 0
            for xl, wl in ((0, 0), (1, 0), (0, 1)):
                for dc in range(DC):
                    k += 1
                    p.op("tensor", sT, "matmul", ps,
                         wqk[mi][wl][:, dc * 128:dc * 128 + 128],
                         xs_b[bufi][xl][:, dc * 512:dc * 512 + w],
                         start=(k == 1), stop=(k == 24))
            mm_done.append(sT.v)
            p.wait("vector", sT, sT.v)
            bias = kbp if kind == "k" else qbp
            p.op("vector", sV, "tensor_scalar", kqstage[:, 0:w], ps,
                 bias[:, pi_:pi_ + 1], None, OP.add)
            if kind == "k":
                split(kqstage[:, 0:512], kf_h[:, o:o + 512],
                      kf_l[:, o:o + 512])
            else:
                split(kqstage[:, 0:w], q_h[:, o:o + w], q_l[:, o:o + w])
            kq_last[(kind, idx % 2)] = sV.v
        kq_done_v = sV.v

        p.wait("tensor", sV, kq_done_v)
        p.wait("tensor", sV, av_last_read)
        fetched = set()
        qblocks = [("sh", 0, 128, [(0, 8)])]
        for j in range(4):
            qblocks.append((j, (4 * j + 1) * 128, 512, klist_for(j)))
        av_read_v = av_last_read
        exp_hist = {}
        eready_v = {}
        eready_g = {}
        avsum_hist = {}
        for qi, (qb, qo, qw, klist) in enumerate(qblocks):
            p.wait("gpsimd", sT, sT.v)
            p.wait("vector", sT, sT.v)
            p.wait("sync", sT, vb_last_use)
            for (g0, g1) in vfetch_ranges[qb]:
                for lvl, dst in ((0, vb_h), (1, vb_l)):
                    p.dma("sync", sVBf[qi % 2], dst[:, g0 * 128:g1 * 128].rearrange(
                        "p (g c) -> p g c", g=g1 - g0),
                        d_vtm[lvl][g0:g1, :, pi_ * 128:pi_ * 128 + 128].rearrange(
                            "g p c -> p g c"))
            p.wait("tensor", sVBf[qi % 2], sVBf[qi % 2].v)
            nk = len(klist)
            LAG = 2

            def emit_av(tt):
                g2, _m = klist[tt]
                p.wait("tensor", sG, eready_g[tt])
                p.wait("tensor", sV, eready_v[tt])
                if tt == 0:
                    p.wait("tensor", sV, av_read_v)
                for hd in range(2):
                    k = 0
                    avreg = (PS[2][0:64, 0:qw] if hd == 0 else
                             PS[2][64:128, 512:512 + qw])
                    for vx, eoff in ((vb_h, 0), (vb_l, 0), (vb_h, 1024)):
                        k += 1
                        p.op("tensor", sT, "matmul", avreg,
                             vx[:, g2 * 128 + hd * 64:g2 * 128 + hd * 64 + 64],
                             ehl[tt % 3][:, eoff + hd * 512:
                                         eoff + hd * 512 + qw],
                             start=(tt == 0 and k == 1),
                             stop=(tt == nk - 1 and k == 3),
                             tile_position=(0, hd * 64))
                    for li_, eoff in ((0, 0), (1, 1024)):
                        p.op("tensor", sT, "matmul",
                             PS[3][hd * 64:hd * 64 + 1, 0:qw],
                             c_ones1[:, 0:1],
                             ehl[tt % 3][:, eoff + hd * 512:
                                         eoff + hd * 512 + qw],
                             start=(tt == 0 and li_ == 0),
                             stop=(tt == nk - 1 and li_ == 1),
                             tile_position=(0, hd * 64))
                avsum_hist[tt] = sT.v

            for t, (g, mslot) in enumerate(klist):
                grp = PS[t % 2]
                if t - 2 in exp_hist:
                    p.wait("tensor", sC, exp_hist[t - 2])
                elif t < 2 and qi > 0:
                    p.wait("tensor", sC, sC.v)
                for hd in range(2):
                    reg = grp[0:128, hd * 512:hd * 512 + qw]
                    k = 0
                    for kx, qx in ((kf_h, q_h), (kf_h, q_l), (kf_l, q_h)):
                        k += 1
                        p.op("tensor", sT, "matmul", reg,
                             kx[hd * 64:hd * 64 + 64, g * 128:g * 128 + 128],
                             qx[hd * 64:hd * 64 + 64, qo:qo + qw],
                             start=(k == 1), stop=(k == 3 and mslot is None),
                             tile_position=(hd * 64, 0))
                    if mslot is not None:
                        p.op("tensor", sT, "matmul", reg,
                             identb[:],
                             masks[:, mslot * 512:mslot * 512 + qw],
                             start=False, stop=True)
                tqk = sT.v
                if t - LAG >= 0:
                    emit_av(t - LAG)
                p.wait("scalar", sT, tqk)
                if t - 3 in eready_v:
                    p.wait("scalar", sV, eready_v[t - 3])
                    p.wait("scalar", sG, eready_g[t - 3])
                for hd in range(2):
                    p.op("scalar", sC, "activation",
                         ef32[t % 3][:, hd * 512:hd * 512 + qw],
                         grp[0:128, hd * 512:hd * 512 + qw], AF.Exp,
                         bias=c_expb[:, 0:1], scale=0.125)
                exp_hist[t] = sC.v
                p.wait("gpsimd", sC, exp_hist[t])
                if t - 3 in avsum_hist:
                    p.wait("gpsimd", sT, avsum_hist[t - 3])
                    p.wait("vector", sT, avsum_hist[t - 3])
                for hd in range(2):
                    p.op("gpsimd", sG, "tensor_copy",
                         ehl[t % 3][:, hd * 512:hd * 512 + qw],
                         ef32[t % 3][:, hd * 512:hd * 512 + qw])
                eready_g[t] = sG.v
                p.wait("vector", sC, exp_hist[t])
                p.wait("vector", sG, eready_g[t])
                for hd in range(2):
                    p.op("vector", sV, "tensor_tensor",
                         ehl[t % 3][:, 1024 + hd * 512:1024 + hd * 512 + qw],
                         ef32[t % 3][:, hd * 512:hd * 512 + qw],
                         ehl[t % 3][:, hd * 512:hd * 512 + qw], OP.subtract)
                eready_v[t] = sV.v
            for tt in range(max(0, nk - LAG), nk):
                emit_av(tt)
            vb_last_use = sT.v
            core_done = sT.v
            p.wait("vector", sT, core_done)
            p.op("vector", sV, "tensor_copy", av_f[0:64, 0:qw], PS[2][0:64, 0:qw])
            p.op("vector", sV, "tensor_copy", av_f[64:128, 0:qw],
                 PS[2][64:128, 512:512 + qw])
            p.op("vector", sV, "tensor_copy", srow[0:1, 0:qw], PS[3][0:1, 0:qw])
            p.op("vector", sV, "tensor_copy", srow[64:65, 0:qw], PS[3][64:65, 0:qw])
            p.op("vector", sV, "reciprocal", srow[0:1, 0:qw], srow[0:1, 0:qw])
            p.op("vector", sV, "reciprocal", srow[64:65, 0:qw], srow[64:65, 0:qw])
            vrec = sV.v
            p.wait("tensor", sV, vrec)
            for hd in range(2):
                p.op("tensor", sT, "matmul", PS[3][hd * 64:hd * 64 + 64,
                                                   512:512 + qw],
                     c_onesf[hd * 64:hd * 64 + 1, 0:64],
                     srow[hd * 64:hd * 64 + 1, 0:qw],
                     start=True, stop=True, tile_position=(hd * 64, hd * 64))
            trep = sT.v
            p.wait("vector", sT, trep)
            p.op("vector", sV, "tensor_tensor", avn_f[:, 0:qw], av_f[:, 0:qw],
                 PS[3][0:128, 512:512 + qw], OP.mult)
            if qi == 0:
                p.wait("vector", sSP, sSP.v)
            split(avn_f[:, 0:qw], av_h[:, qo:qo + qw], av_l[:, qo:qo + qw])
            av_read_v = sV.v
        av_last_read = av_read_v
        p.wait("scalar", sV, av_read_v)
        p.dma("scalar", sSP, d_avsp[0][pi_], av_h[:])
        p.dma("scalar", sSP, d_avsp[1][pi_], av_l[:])
    avsp_done = sSP.v
    ckpts["att"] = p.checkpoint(ALLSEMS)

    barrier()
    wor = d_wos
    x1r = d_x1sp
    evac2 = []
    o_last = {}
    for duo in range(4):
        ldv = []
        p.wait("sync", sT, sT.v)
        for i in range(2):
            pi_ = 2 * duo + i
            for lvl in range(2):
                ldv.append(p.dma("sync", sOAV, oav_h[i][:] if lvl == 0 else
                                 oav_l[i][:], d_avsp[lvl][pi_]))
                ldv.append(p.dma("sync", sOAV, owo[i][lvl][:],
                                 wor[lvl][pi_ * 128:pi_ * 128 + 128, :]))
        for ti in range(NT):
            ps = PS[ti % 2]
            if duo == 0:
                if ("x1", ti % 2) in o_last:
                    p.wait("sync", sV, o_last[("x1", ti % 2)])
                x1ld = p.dma("sync", sX1[ti % 2], x1st[ti % 2][:, 0:DM], x1r[ti])
            if ti % 2 in o_last:
                p.wait("tensor", sV, o_last[ti % 2])
            p.wait("tensor", sOAV, ldv[-1])
            for half in range(2):
                k = 0
                last = 6 + (2 if duo == 0 else 0)
                for i in range(2):
                    for ax, wl in ((oav_h[i], 0), (oav_l[i], 0), (oav_h[i], 1)):
                        k += 1
                        p.op("tensor", sT, "matmul",
                             ps[0:128, half * 512:half * 512 + 512],
                             ax[:, ti * 128:ti * 128 + 128],
                             owo[i][wl][:, half * 512:half * 512 + 512],
                             start=(k == 1), stop=(k == last))
                if duo == 0:
                    for lvl in range(2):
                        k += 1
                        p.op("tensor", sT, "matmul",
                             ps[0:128, half * 512:half * 512 + 512],
                             c_onesb[32:33, 0:128],
                             obb[0:1, lvl * DM + half * 512:
                                 lvl * DM + half * 512 + 512],
                             start=False, stop=(k == last),
                             tile_position=(32, 0))
            tmm = sT.v
            p.wait("vector", sT, tmm)
            if duo == 0:
                p.wait("vector", sX1[ti % 2], x1ld)
                p.op("vector", sV, "tensor_tensor",
                     x2_res[:, ti * DM:(ti + 1) * DM], ps[0:128, 0:1024],
                     x1st[ti % 2][:, 0:DM], OP.add)
            else:
                p.op("vector", sV, "tensor_tensor",
                     x2_res[:, ti * DM:(ti + 1) * DM], ps[0:128, 0:1024],
                     x2_res[:, ti * DM:(ti + 1) * DM], OP.add)
            o_last[ti % 2] = sV.v
            o_last[("x1", ti % 2)] = sV.v

    if "x2d" in DBG.get("expose", ()):
        p.wait("scalar", sV, sV.v)
        p.dma("scalar", sSP, d_x2d[:].rearrange("t p d -> p t d"),
              x2_res[:].rearrange("p (t d) -> p t d", t=NT))

    ckpts["o"] = p.checkpoint(ALLSEMS)
    barrier()
    emit_ln_ffn(2)
    p.wait("sync", sSP, sSP.v)
    p.wait("sync", sV, sV.v)
    if DBG.get("stop") in ckpts:
        p.truncate(ckpts[DBG["stop"]])

    with nc.Block() as block:
        for eng in ENGS:
            def mk(e):
                def run(q):
                    for fn in p.cmds[e]:
                        fn(q)
                return run
            getattr(block, eng)(mk(eng))
    return nc, es


def _pack_inputs(inputs, c):
    rb = c % 2
    b = c // 2
    x = np.asarray(inputs["L0_attn_x"])
    tiles = x[b].reshape(32, 128, DM)
    x_own = np.ascontiguousarray(
        np.concatenate([tiles[0:1], tiles[rb::2]], axis=0))
    W1 = np.asarray(inputs["L0_ffn_W1"])
    W2 = np.asarray(inputs["L0_ffn_W2"])
    Qw = np.asarray(inputs["L0_attn_Q_w"])
    Kw = np.asarray(inputs["L0_attn_K_w"])
    Vw = np.asarray(inputs["L0_attn_V_w"])
    Ow = np.asarray(inputs["L0_attn_O_w"])
    m = {
        "x_own": x_own,
        "w1t": np.ascontiguousarray(W1.T),
        "w2t": np.ascontiguousarray(W2.T),
        "b1p": np.ascontiguousarray(
            np.asarray(inputs["L0_ffn_b1"]).reshape(32, 128).T),
        "b2r": np.asarray(inputs["L0_ffn_b2"]).reshape(1, DM),
        "wqp": np.ascontiguousarray(Qw.transpose(1, 0, 2).reshape(DM, DM)),
        "wkp": np.ascontiguousarray(Kw.transpose(1, 0, 2).reshape(DM, DM)),
        "wvp": np.ascontiguousarray(Vw.transpose(1, 0, 2).reshape(DM, DM)),
        "wop": np.ascontiguousarray(Ow.reshape(DM, DM)),
        "qbp": np.ascontiguousarray(
            np.asarray(inputs["L0_attn_Q_b"]).reshape(8, 128).T),
        "kbp": np.ascontiguousarray(
            np.asarray(inputs["L0_attn_K_b"]).reshape(8, 128).T),
        "vbr": np.asarray(inputs["L0_attn_V_b"]).reshape(1, DM),
        "obr": np.asarray(inputs["L0_attn_O_b"]).reshape(1, DM),
        "screp": np.ascontiguousarray(np.broadcast_to(
            np.asarray(inputs["L0_ffn_scale"]).reshape(1, 1), (128, 1))),
        "identb": np.eye(128, dtype=ml_dtypes.bfloat16),
        "qkmask": _mk_masks(rb),
    }
    return m


def _mk_masks(rb):
    kk = np.arange(128).reshape(128, 1)
    qq = np.arange(128).reshape(1, 128)
    msk = np.zeros((9, 128, 512), np.float32)
    for run in range(2):
        for i in range(4):
            slot = run * 4 + i
            for ip in range(4):
                kglob = 128 * (2 * i + run) + kk
                qglob = 128 * (2 * ip + rb) + qq
                msk[slot][:, ip * 128:(ip + 1) * 128] = np.where(
                    kglob > qglob, NEGM, 0.0)
    msk[8][:, 0:128] = np.where(kk > qq, NEGM, 0.0)
    return msk.astype(ml_dtypes.bfloat16)


NEGM = -100000.0
_CACHED = {}


def kernel(**inputs):
    if "nc" not in _CACHED:
        _CACHED["nc"], _CACHED["es"] = build_program()
    nc = _CACHED["nc"]
    in_maps = [_pack_inputs(inputs, c) for c in range(8)]
    res = run_bass_kernel_spmd(nc, in_maps, core_ids=list(range(8)))
    out = np.empty((B, S, DM), np.float32)
    for c in range(8):
        rb, b = c % 2, c // 2
        out[b].reshape(32, 128, DM)[rb::2] = np.asarray(
            res.results[c]["out_own"])
    return out

